# version 02
# baseline (speedup 1.0000x reference)
import contextlib
import numpy as np
import ml_dtypes

import concourse.bass as bass
import concourse.mybir as mybir
from concourse.bass_utils import run_bass_kernel_spmd

F32 = mybir.dt.float32
F16 = mybir.dt.float16
BF16 = mybir.dt.bfloat16
U8 = mybir.dt.uint8
OP = mybir.AluOpType
AF = mybir.ActivationFunctionType
AX = mybir.AxisListType.X

B_CORE = 2
T = 2048
C = 1024
NCT = 8
NCH = 4
NTT = 16
NU = 32
MAGIC = float(np.float32(1.5 * 2 ** 23))
EPS = 1e-5

ENGS = ("sync", "scalar", "vector", "tensor", "gpsimd")


class Sem:
    __slots__ = ("h", "n")

    def __init__(self, h):
        self.h = h
        self.n = 0


class Sched:

    def __init__(self, nc, es):
        self.nc = nc
        self.q = {e: [] for e in ENGS}
        self.cs = {e: Sem(es.enter_context(nc.semaphore(f"cs_{e}"))) for e in ENGS}
        self.ds = {e: Sem(es.enter_context(nc.semaphore(f"ds_{e}"))) for e in
                   ("sync", "scalar", "gpsimd")}

    def wait(self, eng, sem, val, future=False):
        if val <= 0:
            return
        assert future or val <= sem.n, f"wait on future count: {val} > {sem.n}"
        self.q[eng].append(lambda e, s=sem.h, v=val: e.wait_ge(s, v))

    def emit(self, eng, fn, inc=True):
        s = self.cs[eng]
        if inc:
            s.n += 1
            self.q[eng].append(lambda e, f=fn, h=s.h: f(e).then_inc(h, 1))
            return s.n
        self.q[eng].append(lambda e, f=fn: f(e))
        return s.n

    def drain(self, eng):
        self.q[eng].append(lambda e: e.drain())

    def dma(self, eng, out, in_, slow=False):
        s = self.ds[eng]
        s.n += 16
        self.q[eng].append(
            lambda e, o=out, i=in_, h=s.h, sl=slow: e.dma_start(
                out=o, in_=i, allow_slow_non_contiguous=sl).then_inc(h, 16))
        return s.n

    def run(self, block):
        for eng in ENGS:
            deco = getattr(block, eng)
            ops = self.q[eng]

            def body(handle, _ops=ops):
                for f in _ops:
                    f(handle)

            deco(body)


def _consts16():
    return {
        "ident16": np.eye(128, dtype=np.float16),
        "ident32": np.eye(128, dtype=np.float32),
        "onesbf": np.ones((128, 128), dtype=ml_dtypes.bfloat16),
        "ones32": np.ones((128, 128), dtype=np.float32),
        "ones16": np.ones((128, 128), dtype=np.float16),
    }


def build_p1(debug=False):
    nc = bass.Bass("TRN2", target_bir_lowering=False)
    x_d = nc.dram_tensor("x", [B_CORE, T, C], F32, kind="ExternalInput").ap()
    w_d = {w: nc.dram_tensor(f"W{w}", [C, C], F32, kind="ExternalInput").ap()
           for w in "qkvo"}
    b_d = {w: nc.dram_tensor(f"b{w}", [C], F32, kind="ExternalInput").ap()
           for w in "qkvo"}
    id16_d = nc.dram_tensor("ident16", [128, 128], F16, kind="ExternalInput").ap()
    id32_d = nc.dram_tensor("ident32", [128, 128], F32, kind="ExternalInput").ap()
    onesbf_d = nc.dram_tensor("onesbf", [128, 128], BF16, kind="ExternalInput").ap()
    ones32_d = nc.dram_tensor("ones32", [128, 128], F32, kind="ExternalInput").ap()
    y_d = nc.dram_tensor("y_scr", [B_CORE, C, T], F32, kind="ExternalOutput").ap()
    mx_d = nc.dram_tensor("mx", [1, 1], F32, kind="ExternalOutput").ap()
    tern_scr = nc.dram_tensor("tern_scr", [4, 128, 8192], F16, kind="Internal").ap()
    sml_scr = nc.dram_tensor("sml_scr", [1, 4], F32, kind="Internal").ap()
    dbg = {}
    if debug:
        dbg["alp"] = nc.dram_tensor("dbg_alp", [128, 4], F32, kind="ExternalOutput").ap()
        dbg["tsml"] = nc.dram_tensor("dbg_tsml", [4, 16], F32, kind="ExternalOutput").ap()
        dbg["acc"] = nc.dram_tensor("dbg_acc", [128, 16], F32, kind="ExternalOutput").ap()
        dbg["tern"] = nc.dram_tensor("dbg_tern", [128, 8192], F16, kind="ExternalOutput").ap()
        dbg["ln"] = nc.dram_tensor("dbg_ln", [128, 4096], F16, kind="ExternalOutput").ap()
        dbg["qT"] = nc.dram_tensor("dbg_qT", [128, 16384], F16, kind="ExternalOutput").ap()
        dbg["kT"] = nc.dram_tensor("dbg_kT", [128, 16384], F16, kind="ExternalOutput").ap()
        dbg["v"] = nc.dram_tensor("dbg_v", [128, 16384], BF16, kind="ExternalOutput").ap()
        dbg["E"] = nc.dram_tensor("dbg_E", [128, 16384], BF16, kind="ExternalOutput").ap()
        dbg["yacc"] = nc.dram_tensor("dbg_yacc", [128, 16384], F16, kind="ExternalOutput").ap()

    with contextlib.ExitStack() as es:
        S = Sched(nc, es)
        sb = lambda n, sh, dt: es.enter_context(nc.sbuf_tensor(n, sh, dt))
        ps = lambda n, sh, dt: es.enter_context(nc.psum_tensor(n, sh, dt))
        TEN, VEC, ACT = "tensor", "vector", "scalar"

        U0 = sb("U0", [128, 49152], U8)
        wT = {"q": U0[:, 0:16384].bitcast(F16),
              "k": U0[:, 16384:32768].bitcast(F16),
              "v": U0[:, 32768:49152].bitcast(F16)}
        E_ap = U0[:, 0:32768].bitcast(BF16)
        woT = U0[:, 32768:49152].bitcast(F16)

        lnT = sb("lnT", [128, 4096], F16)
        qT = sb("qT", [128, 16384], F16)
        kT = sb("kT", [128, 16384], F16)
        v_sb = sb("v_sb", [128, 16384], BF16)
        y_acc = sb("y_acc", [128, 16384], F16)
        x_stage = sb("x_stage", [128, 2048], F32)
        ln_tok = sb("ln_tok", [128, 2048], F16)
        y_stage = sb("y_stage", [128, 1024], F32)
        bvbc = sb("bvbc", [128, 1024], F16)
        rzb = sb("rzb", [128, 512], F16)
        junk = sb("junk", [128, 512], BF16)
        id16 = sb("id16_s", [128, 128], F16)
        id32 = sb("id32_s", [128, 128], F32)
        onesbf = sb("onesbf_s", [128, 128], BF16)
        ones32 = sb("ones32_s", [128, 128], F32)
        bq_sb = sb("bq_sb", [128, 8], F32)
        bk_sb = sb("bk_sb", [128, 8], F32)
        bo_sb = sb("bo_sb", [128, 8], F32)
        alp = sb("alp", [128, 4], F32)
        sm1 = sb("sm1", [128, 16], F32)
        sm2 = sb("sm2", [128, 16], F32)
        mx_run = sb("mx_run", [128, 1], F32)
        mx_out = sb("mx_out", [1, 1], F32)
        t_sml = sb("t_sml", [1, 64], F32)
        acc = sb("acc", [128, 16], F32)
        cvec = sb("cvec", [128, 4], F32)

        pqkall = ps("pqkall", [128, 2048], F32)
        ex_sem = Sem(es.enter_context(nc.semaphore("ex_sem")))
        ez_sem = Sem(es.enter_context(nc.semaphore("ez_sem")))
        pqk = pqkall[:, 0:1024]
        pqk2 = pqkall[:, 1024:2048]
        pav = ps("pav", [128, 512], F32)
        pz = ps("pz", [128, 1024], F32)
        ptr16 = ps("ptr16", [128, 1024], F16)

        for col, val in ((0, EPS), (1, 0.125), (2, -1.0), (3, -0.5)):
            S.emit(VEC, lambda e, col=col, val=val: e.memset(cvec[:, col:col + 1], val),
                   inc=(col == 3))
        v_cv = S.cs[VEC].n
        S.wait(ACT, S.cs[VEC], v_cv)
        S.dma("sync", id16[:], id16_d[:])
        S.dma("sync", id32[:], id32_d[:])
        S.dma("sync", onesbf[:], onesbf_d[:])
        v_c32 = S.dma("sync", ones32[:], ones32_d[:])
        S.dma("gpsimd", bq_sb[:], b_d["q"].rearrange("(a p) -> p a", p=128), slow=True)
        S.dma("gpsimd", bk_sb[:], b_d["k"].rearrange("(a p) -> p a", p=128), slow=True)
        v_bo = S.dma("gpsimd", bo_sb[:], b_d["o"].rearrange("(a p) -> p a", p=128), slow=True)
        v_bv = S.dma("gpsimd", bvbc[:], b_d["v"][None, :].partition_broadcast(128)[:, 0, :])
        S.wait(VEC, S.ds["gpsimd"], v_bv)
        S.wait(VEC, S.ds["gpsimd"], v_bo)

        if True:
            w_stage = qT[:].bitcast(F32)
            absw = kT[:].bitcast(F32)
            maskb = v_sb[:, 0:8192].bitcast(F16)
            signb = y_acc[:, 0:8192]
            p3scr = v_sb[:, 8192:16384]
            pmisc = pz

            v_abs_prev = v_sg_prev = v_tn_prev = v_tt_prev = 0
            for wi, w in enumerate("qkvo"):
                tb = 16 * wi
                ab = 4 * wi
                ts = lambda j, tb=tb: t_sml[0:1, tb + j: tb + j + 1]
                ac = lambda j, ab=ab: acc[:, ab + j: ab + j + 1]
                S.wait("sync", S.cs[VEC], v_abs_prev)
                S.wait("sync", S.cs[ACT], v_sg_prev)
                v_ld = S.dma("sync", w_stage[:].rearrange("p (a c) -> p a c", c=1024),
                             w_d[w].rearrange("(a p) c -> p a c", p=128))
                S.wait(VEC, S.ds["sync"], v_ld)
                S.emit(VEC, lambda e, ac=ac: e.scalar_tensor_tensor(
                    out=absw[:], in0=w_stage[:], scalar=-1.0, in1=w_stage[:],
                    op0=OP.mult, op1=OP.max, accum_out=ac(0)), inc=False)
                S.drain(VEC)
                v_abs = S.emit(VEC, lambda e, ts=ts, ac=ac: e.tensor_copy(
                    out=ts(8), in_=ac(0)[0:1, :]))
                S.wait(TEN, S.ds["sync"], v_c32)
                S.wait(TEN, S.cs[VEC], v_abs)
                v_t1 = S.emit(TEN, lambda e, ac=ac, wi=wi: e.transpose(
                    pmisc[0:1, 128 * wi: 128 * wi + 128], ac(0), id32[:]))
                S.wait(VEC, S.cs[TEN], v_t1)
                S.emit(VEC, lambda e, ts=ts, wi=wi: e.tensor_reduce(
                    out=ts(0), in_=pmisc[0:1, 128 * wi: 128 * wi + 128], axis=AX,
                    op=OP.add), inc=False)
                S.drain(VEC)
                v_dl = S.emit(VEC, lambda e, ts=ts: e.tensor_scalar(
                    out=ts(1), in0=ts(0),
                    scalar1=float(0.05 / (1024.0 * 1024.0)), scalar2=None, op0=OP.mult))
                S.wait("gpsimd", S.cs[VEC], v_dl)
                v_db0 = S.dma("gpsimd", sml_scr[0:1, 0:1], ts(1))
                S.wait("gpsimd", S.ds["gpsimd"], v_db0)
                v_db = S.dma("gpsimd", ac(3),
                             sml_scr[0:1, 0:1].partition_broadcast(128)[:, 0, :],
                             slow=True)
                S.wait(VEC, S.ds["gpsimd"], v_db)
                S.emit(VEC, lambda e, ac=ac: e.tensor_scalar(
                    out=maskb[:], in0=absw[:], scalar1=ac(3), scalar2=None,
                    op0=OP.is_gt), inc=False)
                S.drain(VEC)
                S.emit(VEC, lambda e, ac=ac: e.tensor_reduce(
                    out=ac(1), in_=maskb[:], axis=AX, op=OP.add), inc=False)
                S.emit(VEC, lambda e, ac=ac: e.scalar_tensor_tensor(
                    out=p3scr[:], in0=absw[:], scalar=1.0, in1=maskb[:],
                    op0=OP.mult, op1=OP.mult, accum_out=ac(2)), inc=False)
                S.drain(VEC)
                v_ms = S.emit(VEC, lambda e, ts=ts, ac=ac: e.tensor_copy(
                    out=ts(9), in_=ac(2)[0:1, :]))
                S.wait(TEN, S.cs[VEC], v_ms)
                S.emit(TEN, lambda e, ac=ac, wi=wi: e.transpose(
                    pmisc[0:1, 512 + 256 * (wi % 2): 512 + 256 * (wi % 2) + 128], ac(1), id32[:]),
                    inc=False)
                v_t2 = S.emit(TEN, lambda e, ac=ac, wi=wi: e.transpose(
                    pmisc[0:1, 512 + 256 * (wi % 2) + 128: 512 + 256 * (wi % 2) + 256], ac(2),
                    id32[:]))
                S.wait(VEC, S.cs[TEN], v_t2)
                S.emit(VEC, lambda e, ts=ts, wi=wi: e.tensor_reduce(
                    out=ts(2), in_=pmisc[0:1, 512 + 256 * (wi % 2): 512 + 256 * (wi % 2) + 128],
                    axis=AX, op=OP.add), inc=False)
                S.emit(VEC, lambda e, ts=ts, wi=wi: e.tensor_reduce(
                    out=ts(3),
                    in_=pmisc[0:1, 512 + 256 * (wi % 2) + 128: 512 + 256 * (wi % 2) + 256],
                    axis=AX, op=OP.add), inc=False)
                S.drain(VEC)
                S.emit(VEC, lambda e, ts=ts: e.tensor_scalar(
                    out=ts(4), in0=ts(2), scalar1=1.0, scalar2=None, op0=OP.max),
                    inc=False)
                S.drain(VEC)
                S.emit(VEC, lambda e, ts=ts: e.reciprocal(out=ts(5), in_=ts(4)),
                       inc=False)
                S.drain(VEC)
                v_al = S.emit(VEC, lambda e, ts=ts: e.tensor_tensor(
                    out=ts(6), in0=ts(3), in1=ts(5), op=OP.mult))
                S.wait("gpsimd", S.cs[VEC], v_al)
                v_al0 = S.dma("gpsimd", sml_scr[0:1, 1:2], ts(6))
                S.wait("gpsimd", S.ds["gpsimd"], v_al0)
                v_alp = S.dma("gpsimd", alp[:, wi:wi + 1],
                              sml_scr[0:1, 1:2].partition_broadcast(128)[:, 0, :],
                              slow=True)
                S.wait(ACT, S.ds["sync"], v_ld)
                S.wait(ACT, S.cs[VEC], v_tn_prev)
                S.wait(ACT, S.cs[TEN], v_tt_prev)
                v_sg = S.emit(ACT, lambda e: e.activation(out=signb[:],
                                                          in_=w_stage[:], func=AF.Sign))
                S.wait(VEC, S.cs[ACT], v_sg)
                v_tn = S.emit(VEC, lambda e: e.scalar_tensor_tensor(
                    out=signb[:], in0=signb[:], scalar=1.0, in1=maskb[:],
                    op0=OP.mult, op1=OP.mult))
                dst = wT[w] if w != "o" else absw[:, 0:4096].bitcast(F16)
                S.wait(TEN, S.cs[VEC], v_tn)
                for ot in range(8):
                    v_tt = S.emit(TEN, lambda e, ot=ot: [
                        e.transpose(ptr16[:, ct * 128:(ct + 1) * 128],
                                    signb[:, ot * 1024 + ct * 128: ot * 1024 + (ct + 1) * 128],
                                    id16[:]) for ct in range(8)][-1])
                    S.wait(VEC, S.cs[TEN], v_tt)
                    v_ev = S.emit(VEC, lambda e, ot=ot, dst=dst: e.tensor_copy(
                        out=dst.rearrange("p (ct f) -> p ct f", f=1024)[:, :, ot * 128:(ot + 1) * 128],
                        in_=ptr16[:].rearrange("p (ct f) -> p ct f", f=128)))
                    S.wait(TEN, S.cs[VEC], v_ev)
                S.wait("sync", S.cs[VEC], v_ev)
                v_spill = S.dma("sync", tern_scr[wi], dst)
                if debug:
                    S.wait("sync", S.ds["gpsimd"], v_alp)
                    S.dma("sync", dbg["tsml"][wi:wi + 1, :], t_sml[0:1, tb:tb + 16])
                    S.dma("sync", dbg["acc"][:, 4 * wi: 4 * wi + 4], acc[:, ab:ab + 4])
                if debug and w == "q":
                    S.dma("sync", dbg["tern"], dst)
                if debug and w == "o":
                    S.dma("sync", dbg["alp"], alp[:])
                v_abs_prev, v_sg_prev, v_tn_prev, v_tt_prev = v_abs, v_sg, v_tn, v_tt
        S.wait(VEC, S.ds["gpsimd"], v_alp)
        v_mx0 = S.emit(VEC, lambda e: e.memset(mx_run[:], 0.0))
        v_bvb = v_mx0

        norm_mark = {}
        sq_mark = {}
        ptr_evac = {}
        proj_evac = {}
        qk_done = {}
        tt_done = {}
        ln_done = {}
        av_done = {}
        zlast = {}
        wo_mark = {}
        vproj_vec_of = {}

        def emit_avz(uu, s0, s1):
            b_, u_ = divmod(uu, NU)
            p_, c_ = divmod(u_, NCH)
            hA, hB = 2 * p_, 2 * p_ + 1
            for st in range(s0, s1):
                S.wait(TEN, ex_sem, uu * NTT + st + 1)
                if st == 0:
                    S.wait(TEN, S.cs[VEC], tt_done.get(uu - 1, 0))
                    S.wait(TEN, S.cs[ACT], ln_done.get(uu - 1, 0))
                    if u_ == 0:
                        S.wait(TEN, S.cs[VEC], vproj_vec_of[b_])
                S.emit(TEN, lambda e, st=st, hA=hA: e.matmul(
                    pav[0:64, 0:512],
                    v_sb[:, st * 1024 + 64 * hA: st * 1024 + 64 * hA + 64],
                    E_ap[:, st * 512:(st + 1) * 512],
                    start=(st == 0), stop=(st == NTT - 1), tile_position=(0, 0)),
                    inc=False)
                S.emit(TEN, lambda e, st=st, hB=hB: e.matmul(
                    pav[64:128, 0:512],
                    v_sb[:, st * 1024 + 64 * hB: st * 1024 + 64 * hB + 64],
                    E_ap[:, 8192 + st * 512: 8192 + (st + 1) * 512],
                    start=(st == 0), stop=(st == NTT - 1), tile_position=(0, 64),
                    skip_group_check=True), inc=(st == NTT - 1))
                S.emit(TEN, lambda e, st=st: e.matmul(
                    pz[0:64, 0:512], onesbf[:, 0:64], E_ap[:, st * 512:(st + 1) * 512],
                    start=(st == 0), stop=(st == NTT - 1), tile_position=(0, 0)),
                    inc=False)
                ez_sem.n += 1
                S.q[TEN].append(lambda e, st=st, h=ez_sem.h: e.matmul(
                    pz[64:128, 0:512], onesbf[:, 0:64],
                    E_ap[:, 8192 + st * 512: 8192 + (st + 1) * 512],
                    start=(st == 0), stop=(st == NTT - 1), tile_position=(0, 64),
                    skip_group_check=True).then_inc(h, 1))
            if s1 == NTT:
                av_done[uu] = S.cs[TEN].n
                zlast[uu] = S.emit(TEN, lambda e: e.nop())

        def emit_lnrz(uu):
            b_, u_ = divmod(uu, NU)
            p_, c_ = divmod(u_, NCH)
            yo = p_ * 2048 + c_ * 512
            S.wait(ACT, S.cs[TEN], zlast[uu])
            S.wait(ACT, S.cs[VEC], tt_done.get(uu - 1, 0))
            S.emit(ACT, lambda e: e.activation(
                out=pz[0:64, 512:1024], in_=pz[0:64, 0:512], func=AF.Ln), inc=False)
            S.emit(ACT, lambda e: e.activation(
                out=pz[64:128, 512:1024], in_=pz[64:128, 0:512], func=AF.Ln),
                inc=False)
            S.drain(ACT)
            ln_done[uu] = S.emit(ACT, lambda e: e.activation(
                out=rzb[:], in_=pz[:, 512:1024], func=AF.Exp, scale=cvec[:, 2:3]))
            S.wait(VEC, S.cs[ACT], ln_done[uu])
            S.wait(VEC, S.cs[TEN], av_done[uu])
            S.emit(VEC, lambda e, yo=yo: e.tensor_tensor(
                out=y_acc[0:64, yo:yo + 512], in0=pav[0:64, 0:512],
                in1=rzb[0:64, :], op=OP.mult), inc=False)
            tt_done[uu] = S.emit(VEC, lambda e, yo=yo: e.tensor_tensor(
                out=y_acc[64:128, yo:yo + 512], in0=pav[64:128, 0:512],
                in1=rzb[64:128, :], op=OP.mult))

        for b in range(B_CORE):
            if b > 0:
                S.wait("sync", S.cs[TEN], S.cs[TEN].n)
                S.wait("sync", S.cs[ACT], S.cs[ACT].n)
                for wi, w in enumerate("qkv"):
                    reload_v = S.dma("sync", wT[w], tern_scr[wi])
                S.wait(TEN, S.cs[ACT], S.cs[ACT].n)
                S.wait(TEN, S.cs[VEC], S.cs[VEC].n)
            else:
                S.wait(TEN, S.cs[VEC], v_bvb)

            for c in range(NCH):
                for tt in range(4):
                    g = c * 4 + tt
                    gg = b * NTT + g
                    par = g % 2
                    S.wait("gpsimd", S.cs[VEC], norm_mark.get(gg - 2, 0))
                    S.wait("gpsimd", S.cs[ACT], sq_mark.get(gg - 2, 0))
                    v_x = S.dma("gpsimd", x_stage[:, par * 1024:(par + 1) * 1024],
                                x_d[b, g * 128:(g + 1) * 128, :])
                    S.wait(ACT, S.ds["gpsimd"], v_x)
                    S.emit(ACT, lambda e, par=par: [
                        e.activation(out=junk[:],
                                     in_=x_stage[:, par * 1024: par * 1024 + 512],
                                     func=AF.Square,
                                     accum_out=sm1[:, par * 8: par * 8 + 1]),
                        e.activation(out=junk[:],
                                     in_=x_stage[:, par * 1024 + 512:(par + 1) * 1024],
                                     func=AF.Square,
                                     accum_out=sm1[:, par * 8 + 1: par * 8 + 2])][-1],
                        inc=False)
                    S.drain(ACT)
                    sq_mark[gg] = S.emit(ACT, lambda e, par=par: e.copy(
                        out=sm1[:, par * 8 + 7: par * 8 + 8],
                        in_=sm1[:, par * 8 + 1: par * 8 + 2]))
                    S.wait(VEC, S.ds["gpsimd"], v_x)
                    S.emit(VEC, lambda e, par=par: e.tensor_reduce(
                        out=sm2[:, par * 8: par * 8 + 1],
                        in_=x_stage[:, par * 1024:(par + 1) * 1024], axis=AX,
                        op=OP.add), inc=False)
                    S.drain(VEC)
                    S.emit(VEC, lambda e, par=par: e.tensor_scalar(
                        out=sm2[:, par * 8 + 1: par * 8 + 2],
                        in0=sm2[:, par * 8: par * 8 + 1],
                        scalar1=1.0 / 1024.0, scalar2=None, op0=OP.mult), inc=False)
                    S.drain(VEC)
                    S.emit(VEC, lambda e, par=par: e.tensor_tensor(
                        out=sm2[:, par * 8 + 2: par * 8 + 3],
                        in0=sm2[:, par * 8 + 1: par * 8 + 2],
                        in1=sm2[:, par * 8 + 1: par * 8 + 2], op=OP.mult), inc=False)
                    S.wait(VEC, S.cs[ACT], sq_mark[gg])
                    S.emit(VEC, lambda e, par=par: e.tensor_tensor(
                        out=sm1[:, par * 8 + 2: par * 8 + 3],
                        in0=sm1[:, par * 8: par * 8 + 1],
                        in1=sm1[:, par * 8 + 1: par * 8 + 2], op=OP.add), inc=False)
                    S.drain(VEC)
                    v_var = S.emit(VEC, lambda e, par=par: e.scalar_tensor_tensor(
                        out=sm2[:, par * 8 + 3: par * 8 + 4],
                        in0=sm1[:, par * 8 + 2: par * 8 + 3], scalar=1.0 / 1024.0,
                        in1=sm2[:, par * 8 + 2: par * 8 + 3],
                        op0=OP.mult, op1=OP.subtract))
                    S.wait(ACT, S.cs[VEC], v_var)
                    S.emit(ACT, lambda e, par=par: e.activation(
                        out=sm2[:, par * 8 + 4: par * 8 + 5],
                        in_=sm2[:, par * 8 + 3: par * 8 + 4], func=AF.Ln,
                        bias=cvec[:, 0:1]), inc=False)
                    S.drain(ACT)
                    v_rs = S.emit(ACT, lambda e, par=par: e.activation(
                        out=sm2[:, par * 8 + 5: par * 8 + 6],
                        in_=sm2[:, par * 8 + 4: par * 8 + 5], func=AF.Exp,
                        scale=cvec[:, 3:4]))
                    S.wait(VEC, S.cs[ACT], v_rs)
                    S.emit(VEC, lambda e, par=par: e.tensor_tensor(
                        out=sm2[:, par * 8 + 6: par * 8 + 7],
                        in0=sm2[:, par * 8 + 1: par * 8 + 2],
                        in1=sm2[:, par * 8 + 5: par * 8 + 6], op=OP.mult), inc=False)
                    S.drain(VEC)
                    S.wait(VEC, S.cs[TEN], ptr_evac.get(gg - 2, (0, 0))[0])
                    norm_mark[gg] = S.emit(VEC, lambda e, par=par: e.tensor_scalar(
                        out=ln_tok[:, par * 1024:(par + 1) * 1024],
                        in0=x_stage[:, par * 1024:(par + 1) * 1024],
                        scalar1=sm2[:, par * 8 + 5: par * 8 + 6],
                        scalar2=sm2[:, par * 8 + 6: par * 8 + 7],
                        op0=OP.mult, op1=OP.subtract))
                    S.wait(TEN, S.cs[VEC], norm_mark[gg])
                    if debug and gg == 4 and "v_ln" in dbg:
                        S.wait(TEN, S.ds["sync"], dbg["v_ln"])
                    v_tp = S.emit(TEN, lambda e, par=par: [
                        e.transpose(ptr16[:, ct * 128:(ct + 1) * 128],
                                    ln_tok[:, par * 1024 + ct * 128: par * 1024 + (ct + 1) * 128],
                                    id16[:]) for ct in range(8)][-1])
                    S.wait(VEC, S.cs[TEN], v_tp)
                    v_pe = S.emit(VEC, lambda e, tt=tt: e.tensor_copy(
                        out=lnT.rearrange("p (ct f) -> p ct f", f=512)[:, :, tt * 128:(tt + 1) * 128],
                        in_=ptr16[:].rearrange("p (ct f) -> p ct f", f=128)))
                    ptr_evac[gg] = (v_tp, v_pe)
                    S.wait(TEN, S.cs[VEC], v_pe)
                v_lnT = ptr_evac[b * NTT + c * 4 + 3][1]
                if debug and b == 0 and c == 0:
                    S.wait("sync", S.cs[VEC], v_lnT)
                    dbg["v_ln"] = S.dma("sync", dbg["ln"], lnT[:])
                S.wait(TEN, S.cs[VEC], v_lnT)
                if b > 0 and c == 0:
                    S.wait(TEN, S.ds["sync"], reload_v)
                for w, dstT, psum, bias, ai in (("q", qT, pqk, bq_sb, 0),
                                                ("k", kT, pqk2, bk_sb, 1)):
                    for ot in range(8):
                        jj = (b * NCH + c) * 8 + ot
                        half = ot % 2
                        S.wait(TEN, S.cs[VEC], proj_evac.get((w, jj - 2), 0))
                        for ct in range(8):
                            vv = S.emit(TEN, lambda e, ct=ct, ot=ot, half=half, w=w, psum=psum: e.matmul(
                                psum[:, half * 512:(half + 1) * 512],
                                wT[w][:, ct * 1024 + ot * 128: ct * 1024 + (ot + 1) * 128],
                                lnT[:, ct * 512:(ct + 1) * 512],
                                start=(ct == 0), stop=(ct == 7)), inc=(ct == 7))
                        S.wait(VEC, S.cs[TEN], vv)
                        proj_evac[(w, jj)] = S.emit(
                            VEC, lambda e, ot=ot, half=half, dstT=dstT, psum=psum,
                            bias=bias, ai=ai, c=c: e.tensor_scalar(
                                out=dstT[:, ot * 2048 + c * 512: ot * 2048 + (c + 1) * 512],
                                in0=psum[:, half * 512:(half + 1) * 512],
                                scalar1=alp[:, ai:ai + 1], scalar2=bias[:, ot:ot + 1],
                                op0=OP.mult, op1=OP.add))
                for ts4 in range(4):
                    st_g = c * 4 + ts4
                    for dc in range(2):
                        jj = (b * NCH + c) * 8 + ts4 * 2 + dc
                        half = (ts4 * 2 + dc) % 2
                        S.wait(TEN, S.cs[VEC], proj_evac.get(("v", jj - 2), 0))
                        if b > 0 and c == 0 and ts4 == 0 and dc == 0:
                            pass
                        for ct in range(8):
                            vv = S.emit(TEN, lambda e, ct=ct, ts4=ts4, dc=dc, half=half: e.matmul(
                                pz[:, half * 512:(half + 1) * 512],
                                lnT[:, ct * 512 + ts4 * 128: ct * 512 + (ts4 + 1) * 128],
                                wT["v"][:, ct * 1024 + dc * 512: ct * 1024 + (dc + 1) * 512],
                                start=(ct == 0), stop=(ct == 7)), inc=(ct == 7))
                        S.wait(VEC, S.cs[TEN], vv)
                        proj_evac[("v", jj)] = S.emit(
                            VEC, lambda e, st_g=st_g, dc=dc, half=half:
                            e.scalar_tensor_tensor(
                                out=v_sb[:, st_g * 1024 + dc * 512: st_g * 1024 + (dc + 1) * 512],
                                in0=pz[:, half * 512:(half + 1) * 512],
                                scalar=alp[:, 2:3],
                                in1=bvbc[:, dc * 512:(dc + 1) * 512],
                                op0=OP.mult, op1=OP.add))
            v_proj_pe = S.cs[TEN].n
            v_proj_vec = S.cs[VEC].n
            if debug and b == 0:
                S.wait("sync", S.cs[VEC], v_proj_vec)
                S.dma("sync", dbg["qT"], qT[:])
                S.dma("sync", dbg["kT"], kT[:])
                S.dma("sync", dbg["v"], v_sb[:])

            S.wait("sync", S.cs[VEC], v_proj_vec)
            v_wo = S.dma("sync", woT, tern_scr[3])

            vproj_vec_of[b] = v_proj_vec
            def emit_qk(uu, s0, s1):
                b_, u_ = divmod(uu, NU)
                p, c = divmod(u_, NCH)
                for st in range(s0, s1):
                    ss = uu * NTT + st
                    buf = pqk if ss % 2 == 0 else pqk2
                    S.wait(TEN, ex_sem, ss - 1, future=True)
                    if u_ == 0 and st == 0:
                        S.wait(TEN, S.cs[VEC], vproj_vec_of[b_])
                    S.emit(TEN, lambda e, st=st, p=p, c=c, buf=buf: e.matmul(
                        buf[:, 0:512],
                        kT[0:64, p * 2048 + st * 128: p * 2048 + (st + 1) * 128],
                        qT[0:64, p * 2048 + c * 512: p * 2048 + (c + 1) * 512],
                        start=True, stop=True, tile_position=(0, 0)), inc=False)
                    qk_done[(uu, st)] = S.emit(TEN, lambda e, st=st, p=p, c=c, buf=buf: e.matmul(
                        buf[:, 512:1024],
                        kT[64:128, p * 2048 + st * 128: p * 2048 + (st + 1) * 128],
                        qT[64:128, p * 2048 + c * 512: p * 2048 + (c + 1) * 512],
                        start=True, stop=True, tile_position=(64, 0)))

            for u in range(NU):
                uu = b * NU + u
                if u > 0:
                    emit_avz(uu - 1, 0, 8)
                emit_qk(uu, 0, 8)
                if u > 0:
                    emit_avz(uu - 1, 8, NTT)
                emit_qk(uu, 8, NTT)
                for st in range(NTT):
                    buf = pqk if (uu * NTT + st) % 2 == 0 else pqk2
                    S.wait(ACT, S.cs[TEN], qk_done[(uu, st)])
                    S.wait(ACT, ez_sem, (uu - 1) * NTT + st + 1)
                    if u == 0:
                        S.wait(ACT, S.cs[TEN], v_proj_pe)
                        S.wait(ACT, S.ds["sync"], v_spill)
                    ex_sem.n += 1
                    S.q[ACT].append(lambda e, st=st, buf=buf, h=ex_sem.h: e.activation(
                        out=E_ap.rearrange("p (h f) -> p h f", f=8192)[:, :, st * 512:(st + 1) * 512],
                        in_=buf[:], func=AF.Exp, scale=cvec[:, 1:2]).then_inc(h, 1))
                if u > 0:
                    emit_lnrz(uu - 1)
            emit_avz(b * NU + NU - 1, 0, NTT)
            emit_lnrz(b * NU + NU - 1)
            if debug and b == 0:
                S.wait("sync", S.cs[VEC], tt_done[b * NU + NU - 1])
                S.dma("sync", dbg["yacc"], y_acc[:])
            S.wait(TEN, S.ds["sync"], v_wo)
            S.wait(TEN, S.cs[ACT], S.cs[ACT].n)
            S.wait(TEN, ex_sem, (b * NU + NU) * NTT)
            S.wait(TEN, S.cs[VEC], tt_done[b * NU + NU - 1])
            for ot in range(8):
                for c in range(NCH):
                    j = b * 32 + ot * NCH + c
                    buf = pqk if j % 2 == 0 else pqk2
                    S.wait(TEN, S.cs[VEC], wo_mark.get((j - 2, "ev"), 0))
                    for ct in range(8):
                        vv = S.emit(TEN, lambda e, ct=ct, ot=ot, c=c, buf=buf: e.matmul(
                            buf[:, 0:512],
                            woT[:, ct * 1024 + ot * 128: ct * 1024 + (ot + 1) * 128],
                            y_acc[:, ct * 2048 + c * 512: ct * 2048 + (c + 1) * 512],
                            start=(ct == 0), stop=(ct == 7)), inc=(ct == 7))
                    S.wait(VEC, S.cs[TEN], vv)
                    S.wait(VEC, S.ds["sync"], wo_mark.get((j - 2, "dma"), 0))
                    half = j % 2
                    S.emit(VEC, lambda e, half=half, ot=ot, buf=buf: e.tensor_scalar(
                        out=y_stage[:, half * 512:(half + 1) * 512], in0=buf[:, 0:512],
                        scalar1=alp[:, 3:4], scalar2=bo_sb[:, ot:ot + 1],
                        op0=OP.mult, op1=OP.add), inc=False)
                    S.drain(VEC)
                    S.emit(VEC, lambda e, half=half: e.reduce_max(
                        out=sm1[:, 15:16], in_=y_stage[:, half * 512:(half + 1) * 512],
                        axis=AX, apply_absolute_value=True), inc=False)
                    S.drain(VEC)
                    wo_mark[(j, "ev")] = S.emit(VEC, lambda e: e.tensor_tensor(
                        out=mx_run[:], in0=mx_run[:], in1=sm1[:, 15:16], op=OP.max))
                    S.wait("sync", S.cs[VEC], wo_mark[(j, "ev")])
                    wo_mark[(j, "dma")] = S.dma(
                        "sync", y_d[b, ot * 128:(ot + 1) * 128, c * 512:(c + 1) * 512],
                        y_stage[:, half * 512:(half + 1) * 512])

        S.wait(TEN, S.cs[VEC], S.cs[VEC].n)
        S.wait(TEN, S.cs[ACT], S.cs[ACT].n)
        v_tm = S.emit(TEN, lambda e: e.transpose(pz[0:1, 0:128], mx_run[:], id32[:]))
        S.wait(VEC, S.cs[TEN], v_tm)
        v_mx = S.emit(VEC, lambda e: e.tensor_reduce(
            out=mx_out[:], in_=pz[0:1, 0:128], axis=AX, op=OP.max))
        S.wait("sync", S.cs[VEC], v_mx)
        S.dma("sync", mx_d[:], mx_out[:])
        S.wait("sync", S.ds["sync"], S.ds["sync"].n)

        with nc.Block() as block:
            S.run(block)
    return nc


def build_p2():
    nc = bass.Bass("TRN2", target_bir_lowering=False)
    x_d = nc.dram_tensor("x", [B_CORE, T, C], F32, kind="ExternalInput").ap()
    y_d = nc.dram_tensor("y_scr", [B_CORE, C, T], F32, kind="ExternalInput").ap()
    am_d = nc.dram_tensor("allmax", [1, 8], F32, kind="ExternalInput").ap()
    wf_d = {w: nc.dram_tensor(f"W_{w}", [C, C], F32, kind="ExternalInput").ap()
            for w in "rkv"}
    mur_d = nc.dram_tensor("mu_r", [C], F32, kind="ExternalInput").ap()
    muk_d = nc.dram_tensor("mu_k", [C], F32, kind="ExternalInput").ap()
    id16_d = nc.dram_tensor("ident16", [128, 128], F16, kind="ExternalInput").ap()
    id32_d = nc.dram_tensor("ident32", [128, 128], F32, kind="ExternalInput").ap()
    ones16_d = nc.dram_tensor("ones16", [128, 128], F16, kind="ExternalInput").ap()
    out_d = nc.dram_tensor("out", [B_CORE, T, C], F32, kind="ExternalOutput").ap()

    with contextlib.ExitStack() as es:
        S = Sched(nc, es)
        sb = lambda n, sh, dt: es.enter_context(nc.sbuf_tensor(n, sh, dt))
        ps = lambda n, sh, dt: es.enter_context(nc.psum_tensor(n, sh, dt))
        TEN, VEC, ACT = "tensor", "vector", "scalar"

        wfT = {w: sb(f"w{w}T", [128, 8192], F16) for w in "rkv"}
        ln2 = sb("ln2", [128, 16392], F16)
        yb = sb("yb", [128, 4096], F32)
        xT = sb("xT", [128, 4096], F32)
        x2T = sb("x2T", [128, 4096], F32)
        xr = sb("xr", [128, 4096], F16)
        xk = sb("xk", [128, 4096], F16)
        r_buf = sb("r_buf", [128, 4096], F16)
        ka_buf = sb("ka_buf", [128, 4096], F16)
        krelu = sb("krelu", [128, 512], F16)
        x2h = sb("x2h", [128, 512], F16)
        xsqh = sb("xsqh", [128, 512], F16)
        stat = sb("stat", [128, 2560], F32)
        fin = sb("fin", [128, 2048], F32)
        out_stage = sb("out_stage", [128, 4096], F32)
        x_stage = sb("x_stage", [128, 2048], F32)
        id16 = sb("id16_s", [128, 128], F16)
        id32 = sb("id32_s", [128, 128], F32)
        ones16 = sb("ones16_s", [128, 128], F16)
        amx = sb("amx", [128, 8], F32)
        gmax = sb("gmax", [128, 4], F32)
        cvec = sb("cvec", [128, 4], F32)
        mur = sb("mur_s", [128, 8], F32)
        muk = sb("muk_s", [128, 8], F32)
        omur = sb("omur", [128, 8], F32)
        omuk = sb("omuk", [128, 8], F32)

        ps1 = ps("ps1", [128, 512], F32)
        ps2 = ps("ps2", [128, 512], F32)
        pffn = ps("pffn", [128, 1024], F32)
        ptrp = ps("ptrp", [128, 1024], F32)

        for col, val in ((0, EPS), (1, 0.5), (2, -0.5)):
            S.emit(VEC, lambda e, col=col, val=val: e.memset(cvec[:, col:col + 1], val),
                   inc=(col == 2))
        v_cv = S.cs[VEC].n
        S.wait(ACT, S.cs[VEC], v_cv)
        S.dma("sync", id16[:], id16_d[:])
        S.dma("sync", id32[:], id32_d[:])
        S.dma("sync", ones16[:], ones16_d[:])
        v_am = S.dma("gpsimd", amx[:], am_d[0:1, :].partition_broadcast(128)[:, 0, :])
        S.dma("gpsimd", mur[:], mur_d.rearrange("(a p) -> p a", p=128), slow=True)
        v_mu = S.dma("gpsimd", muk[:], muk_d.rearrange("(a p) -> p a", p=128), slow=True)
        S.wait(VEC, S.ds["gpsimd"], v_am)
        S.emit(VEC, lambda e: e.reduce_max(out=gmax[:, 0:1], in_=amx[:], axis=AX),
               inc=False)
        S.drain(VEC)
        S.emit(VEC, lambda e: e.tensor_scalar(
            out=gmax[:, 1:2], in0=gmax[:, 0:1], scalar1=1.0 / 7.0, scalar2=1e-8,
            op0=OP.mult, op1=OP.max), inc=False)
        S.drain(VEC)
        v_sc = S.emit(VEC, lambda e: e.reciprocal(out=gmax[:, 2:3], in_=gmax[:, 1:2]))
        S.wait(VEC, S.ds["gpsimd"], v_mu)
        S.emit(VEC, lambda e: e.tensor_scalar(
            out=omur[:], in0=mur[:], scalar1=-1.0, scalar2=1.0, op0=OP.mult,
            op1=OP.add), inc=False)
        v_mu2 = S.emit(VEC, lambda e: e.tensor_scalar(
            out=omuk[:], in0=muk[:], scalar1=-1.0, scalar2=1.0, op0=OP.mult,
            op1=OP.add))

        with contextlib.ExitStack() as ses:
            wstage = ln2[:, 0:8192]
            ptr16 = ses.enter_context(nc.psum_tensor("ptr16", [128, 1024], F16))
            ev_mark = 0
            v_dma_prev = 0
            for w in "rkv":
                S.wait("gpsimd", S.cs[TEN], v_dma_prev)
                v_ld = S.dma("gpsimd", wstage.rearrange("p (a c) -> p a c", c=1024),
                             wf_d[w].rearrange("(a p) c -> p a c", p=128))
                S.wait(TEN, S.ds["gpsimd"], v_ld)
                for ot in range(8):
                    v_tt = S.emit(TEN, lambda e, ot=ot: [
                        e.transpose(ptr16[:, ct * 128:(ct + 1) * 128],
                                    wstage[:, ot * 1024 + ct * 128: ot * 1024 + (ct + 1) * 128],
                                    id16[:]) for ct in range(8)][-1])
                    S.wait(VEC, S.cs[TEN], v_tt)
                    ev_mark = S.emit(VEC, lambda e, ot=ot, w=w: e.tensor_copy(
                        out=wfT[w].rearrange("p (ct f) -> p ct f", f=1024)[:, :, ot * 128:(ot + 1) * 128],
                        in_=ptr16[:].rearrange("p (ct f) -> p ct f", f=128)))
                    S.wait(TEN, S.cs[VEC], ev_mark)
                v_dma_prev = S.cs[TEN].n
        v_w_t = S.cs[TEN].n

        x_mark = {}
        yb_free = {}
        x2_free = {}
        fin_mark = {}
        out_ev = {}
        out_dma = {}
        r_psum_rd = {}
        k_psum_rd = {}
        o_psum_rd = {}
        for b in range(B_CORE):
            if b == 0:
                S.wait(VEC, S.cs[TEN], v_w_t)
            v_z0 = S.emit(VEC, lambda e: e.memset(
                ln2.rearrange("p (ct f) -> p ct f", f=2049)[:, :, 0:1], 0.0))
            for c in range(NCH):
                cc = b * NCH + c
                S.wait("sync", S.cs[VEC], yb_free.get(cc - 1, 0))
                v_y = S.dma("sync", yb.rearrange("p (ct f) -> p ct f", f=512),
                            y_d[b].rearrange("(ct p) t -> p ct t", p=128)[:, :, c * 512:(c + 1) * 512])
                for tt in range(4):
                    gg = cc * 4 + tt
                    par = tt % 2
                    S.wait("sync", S.cs[TEN], x_mark.get(gg - 2, 0))
                    v_x = S.dma("sync", x_stage[:, par * 1024:(par + 1) * 1024],
                                x_d[b, (c * 4 + tt) * 128:(c * 4 + tt + 1) * 128, :])
                    S.wait(TEN, S.ds["sync"], v_x)
                    if gg == 0:
                        S.wait(TEN, S.cs[TEN], v_w_t)
                    S.wait(TEN, S.cs[VEC], x2_free.get(cc - 1, 0))
                    S.wait(TEN, S.cs[VEC], out_ev.get((cc - 1) * 8 + 7, 0))
                    x_mark[gg] = S.emit(TEN, lambda e, par=par: [
                        e.transpose(ptrp[:, (ct // 4) * 512 + (ct % 4) * 128:
                                         (ct // 4) * 512 + (ct % 4) * 128 + 128],
                                    x_stage[:, par * 1024 + ct * 128: par * 1024 + (ct + 1) * 128],
                                    id32[:]) for ct in range(8)][-1])
                    S.wait(VEC, S.cs[TEN], x_mark[gg])
                    S.emit(VEC, lambda e, tt=tt: e.tensor_copy(
                        out=xT.rearrange("p (ct f) -> p ct f", f=512)[:, 0:4, tt * 128:(tt + 1) * 128],
                        in_=ptrp[:, 0:512].rearrange("p (ct f) -> p ct f", f=128)),
                        inc=False)
                    v_xe = S.emit(VEC, lambda e, tt=tt: e.tensor_copy(
                        out=xT.rearrange("p (ct f) -> p ct f", f=512)[:, 4:8, tt * 128:(tt + 1) * 128],
                        in_=ptrp[:, 512:1024].rearrange("p (ct f) -> p ct f", f=128)))
                    S.wait(TEN, S.cs[VEC], v_xe)
                S.wait(VEC, S.ds["sync"], v_y)
                S.wait(VEC, S.cs[VEC], v_sc)
                for ct in range(8):
                    sl = slice(ct * 512, (ct + 1) * 512)
                    S.emit(VEC, lambda e, sl=sl: e.tensor_scalar(
                        out=x2T[:, sl], in0=yb[:, sl], scalar1=gmax[:, 2:3],
                        scalar2=MAGIC, op0=OP.mult, op1=OP.add), inc=False)
                S.drain(VEC)
                for ct in range(8):
                    sl = slice(ct * 512, (ct + 1) * 512)
                    S.emit(VEC, lambda e, sl=sl: e.tensor_scalar(
                        out=yb[:, sl], in0=x2T[:, sl], scalar1=MAGIC, scalar2=7.0,
                        op0=OP.subtract, op1=OP.min), inc=False)
                S.drain(VEC)
                for ct in range(8):
                    sl = slice(ct * 512, (ct + 1) * 512)
                    S.emit(VEC, lambda e, sl=sl: e.tensor_scalar(
                        out=yb[:, sl], in0=yb[:, sl], scalar1=-7.0,
                        scalar2=gmax[:, 1:2], op0=OP.max, op1=OP.mult), inc=False)
                S.drain(VEC)
                for ct in range(8):
                    sl = slice(ct * 512, (ct + 1) * 512)
                    S.emit(VEC, lambda e, sl=sl: e.tensor_tensor(
                        out=x2T[:, sl], in0=yb[:, sl], in1=xT[:, sl], op=OP.add),
                        inc=False)
                S.drain(VEC)
                st_mm = 0
                for ct in range(8):
                    sl = slice(ct * 512, (ct + 1) * 512)
                    S.wait(VEC, S.cs[TEN], st_mm)
                    S.emit(VEC, lambda e, sl=sl: e.tensor_copy(out=x2h[:],
                                                               in_=x2T[:, sl]),
                           inc=False)
                    v_st = S.emit(VEC, lambda e, sl=sl: e.tensor_tensor(
                        out=xsqh[:], in0=x2T[:, sl], in1=x2T[:, sl], op=OP.mult))
                    S.wait(TEN, S.cs[VEC], v_st)
                    S.emit(TEN, lambda e, ct=ct: e.matmul(
                        ps1[:], ones16[:], x2h[:], start=(ct == 0), stop=(ct == 7)),
                        inc=False)
                    st_mm = S.emit(TEN, lambda e, ct=ct: e.matmul(
                        ps2[:], ones16[:], xsqh[:], start=(ct == 0), stop=(ct == 7)))
                S.wait(VEC, S.cs[TEN], st_mm)
                S.emit(VEC, lambda e: e.tensor_scalar(
                    out=stat[:, 0:512], in0=ps1[:], scalar1=1.0 / 1024.0,
                    scalar2=None, op0=OP.mult), inc=False)
                S.drain(VEC)
                S.emit(VEC, lambda e: e.tensor_tensor(
                    out=stat[:, 512:1024], in0=stat[:, 0:512], in1=stat[:, 0:512],
                    op=OP.mult), inc=False)
                S.drain(VEC)
                v_var = S.emit(VEC, lambda e: e.scalar_tensor_tensor(
                    out=stat[:, 1024:1536], in0=ps2[:], scalar=1.0 / 1024.0,
                    in1=stat[:, 512:1024], op0=OP.mult, op1=OP.subtract))
                S.wait(ACT, S.cs[VEC], v_var)
                S.emit(ACT, lambda e: e.activation(
                    out=stat[:, 512:1024], in_=stat[:, 1024:1536], func=AF.Ln,
                    bias=cvec[:, 0:1]), inc=False)
                S.drain(ACT)
                v_rstd = S.emit(ACT, lambda e: e.activation(
                    out=stat[:, 1536:2048], in_=stat[:, 512:1024], func=AF.Exp,
                    scale=cvec[:, 2:3]))
                S.wait(VEC, S.cs[ACT], v_rstd)
                v_mrs = S.emit(VEC, lambda e: e.tensor_tensor(
                    out=stat[:, 2048:2560], in0=stat[:, 0:512],
                    in1=stat[:, 1536:2048], op=OP.mult))
                S.drain(VEC)
                for ct in range(8):
                    sl = slice(ct * 512, (ct + 1) * 512)
                    S.emit(VEC, lambda e, sl=sl: e.tensor_tensor(
                        out=yb[:, sl], in0=x2T[:, sl], in1=stat[:, 1536:2048],
                        op=OP.mult), inc=False)
                S.drain(VEC)
                for ct in range(8):
                    sl = slice(ct * 512, (ct + 1) * 512)
                    dsl = slice(ct * 2049 + 1 + c * 512, ct * 2049 + 1 + (c + 1) * 512)
                    v_l2 = S.emit(VEC, lambda e, sl=sl, dsl=dsl: e.tensor_tensor(
                        out=ln2[:, dsl], in0=yb[:, sl], in1=stat[:, 2048:2560],
                        op=OP.subtract), inc=(ct == 7))
                yb_free[cc] = v_l2
                S.wait(VEC, S.cs[VEC], v_z0)
                S.drain(VEC)
                for ct in range(8):
                    psl = slice(ct * 2049 + c * 512, ct * 2049 + (c + 1) * 512)
                    for dst, omu_v in ((xr, omur), (xk, omuk)):
                        S.emit(VEC, lambda e, psl=psl, dst=dst, omu_v=omu_v, ct=ct:
                               e.tensor_scalar(
                                   out=dst[:, ct * 512:(ct + 1) * 512],
                                   in0=ln2[:, psl], scalar1=omu_v[:, ct:ct + 1],
                                   scalar2=None, op0=OP.mult), inc=False)
                S.drain(VEC)
                for ct in range(8):
                    dsl = slice(ct * 2049 + 1 + c * 512, ct * 2049 + 1 + (c + 1) * 512)
                    for dst, mu_v in ((xr, mur), (xk, muk)):
                        v_mix = S.emit(VEC, lambda e, dsl=dsl, dst=dst, mu_v=mu_v, ct=ct:
                                       e.scalar_tensor_tensor(
                                           out=dst[:, ct * 512:(ct + 1) * 512],
                                           in0=ln2[:, dsl], scalar=mu_v[:, ct:ct + 1],
                                           in1=dst[:, ct * 512:(ct + 1) * 512],
                                           op0=OP.mult, op1=OP.add),
                                       inc=(ct == 7 and dst is xk))
                S.wait(TEN, S.cs[VEC], v_mix)
                for dt in range(8):
                    half = dt % 2
                    S.wait(TEN, S.cs[ACT], r_psum_rd.get((cc, dt - 2), 0))
                    for ct in range(8):
                        v_r = S.emit(TEN, lambda e, dt=dt, ct=ct, half=half: e.matmul(
                            pffn[:, half * 512:(half + 1) * 512],
                            wfT["r"][:, ct * 1024 + dt * 128: ct * 1024 + (dt + 1) * 128],
                            xr[:, ct * 512:(ct + 1) * 512],
                            start=(ct == 0), stop=(ct == 7)), inc=(ct == 7))
                    S.wait(ACT, S.cs[TEN], v_r)
                    r_psum_rd[(cc, dt)] = S.emit(ACT, lambda e, half=half: e.activation(
                        out=krelu[:], in_=pffn[:, half * 512:(half + 1) * 512],
                        func=AF.Tanh, scale=cvec[:, 1:2]))
                    S.wait(VEC, S.cs[ACT], r_psum_rd[(cc, dt)])
                    S.emit(VEC, lambda e, dt=dt: e.tensor_scalar(
                        out=r_buf[:, dt * 512:(dt + 1) * 512], in0=krelu[:],
                        scalar1=0.5, scalar2=0.5, op0=OP.mult, op1=OP.add),
                        inc=False)
                S.wait(TEN, S.cs[ACT], r_psum_rd[(cc, 6)])
                S.wait(TEN, S.cs[ACT], r_psum_rd[(cc, 7)])
                for dt in range(8):
                    half = dt % 2
                    S.wait(TEN, S.cs[VEC], k_psum_rd.get((cc, dt - 2), 0))
                    for ct in range(8):
                        v_k = S.emit(TEN, lambda e, dt=dt, ct=ct, half=half: e.matmul(
                            pffn[:, half * 512:(half + 1) * 512],
                            wfT["k"][:, ct * 1024 + dt * 128: ct * 1024 + (dt + 1) * 128],
                            xk[:, ct * 512:(ct + 1) * 512],
                            start=(ct == 0), stop=(ct == 7)), inc=(ct == 7))
                    S.wait(VEC, S.cs[TEN], v_k)
                    S.emit(VEC, lambda e, half=half: e.tensor_scalar(
                        out=krelu[:], in0=pffn[:, half * 512:(half + 1) * 512],
                        scalar1=0.0, scalar2=None, op0=OP.max), inc=False)
                    S.drain(VEC)
                    k_psum_rd[(cc, dt)] = S.emit(VEC, lambda e, dt=dt: e.tensor_tensor(
                        out=ka_buf[:, dt * 512:(dt + 1) * 512], in0=krelu[:],
                        in1=krelu[:], op=OP.mult))
                S.wait(TEN, S.cs[VEC], k_psum_rd[(cc, 6)])
                S.wait(TEN, S.cs[VEC], k_psum_rd[(cc, 7)])
                for dt in range(8):
                    half = dt % 2
                    fpar = dt % 2
                    S.wait(TEN, S.cs[VEC], o_psum_rd.get((cc, dt - 2), 0))
                    for ct in range(8):
                        v_o = S.emit(TEN, lambda e, dt=dt, ct=ct, half=half: e.matmul(
                            pffn[:, half * 512:(half + 1) * 512],
                            wfT["v"][:, ct * 1024 + dt * 128: ct * 1024 + (dt + 1) * 128],
                            ka_buf[:, ct * 512:(ct + 1) * 512],
                            start=(ct == 0), stop=(ct == 7)), inc=(ct == 7))
                    S.wait(VEC, S.cs[TEN], v_o)
                    S.wait(VEC, S.cs[TEN], fin_mark.get((cc, dt - 2), 0))
                    S.emit(VEC, lambda e, dt=dt, half=half, fpar=fpar:
                           e.scalar_tensor_tensor(
                               out=fin[:, fpar * 512:(fpar + 1) * 512],
                               in0=pffn[:, half * 512:(half + 1) * 512], scalar=1.0,
                               in1=r_buf[:, dt * 512:(dt + 1) * 512],
                               op0=OP.mult, op1=OP.mult), inc=False)
                    S.drain(VEC)
                    o_psum_rd[(cc, dt)] = S.emit(VEC, lambda e, dt=dt, fpar=fpar:
                                                 e.tensor_tensor(
                                                     out=fin[:, 1024 + fpar * 512: 1024 + (fpar + 1) * 512],
                                                     in0=fin[:, fpar * 512:(fpar + 1) * 512],
                                                     in1=x2T[:, dt * 512:(dt + 1) * 512],
                                                     op=OP.add))
                    S.wait(TEN, S.cs[VEC], o_psum_rd[(cc, dt)])
                    ff = cc * 8 + dt
                    S.wait(TEN, S.cs[VEC], out_ev.get(ff - 2, 0))
                    ppar = dt % 2
                    fin_mark[(cc, dt)] = S.emit(TEN, lambda e, fpar=fpar, ppar=ppar: [
                        e.transpose(ptrp[:, ppar * 512 + ts * 128: ppar * 512 + (ts + 1) * 128],
                                    fin[:, 1024 + fpar * 512 + ts * 128: 1024 + fpar * 512 + (ts + 1) * 128],
                                    id32[:]) for ts in range(4)][-1])
                    S.wait(VEC, S.cs[TEN], fin_mark[(cc, dt)])
                    S.wait(VEC, S.ds["sync"], out_dma.get(cc - 1, 0))
                    out_ev[ff] = S.emit(VEC, lambda e, dt=dt, ppar=ppar:
                                        e.tensor_copy(
                                            out=out_stage.rearrange("p (ts f) -> p ts f", f=1024)[:, :, dt * 128:(dt + 1) * 128],
                                            in_=ptrp[:, ppar * 512:(ppar + 1) * 512].rearrange("p (ts f) -> p ts f", f=128)))
                x2_free[cc] = o_psum_rd[(cc, 7)]
                S.wait("sync", S.cs[VEC], out_ev[cc * 8 + 7])
                for ts in range(4):
                    dv = S.dma("sync", out_d[b, c * 512 + ts * 128: c * 512 + (ts + 1) * 128, :],
                               out_stage[:, ts * 1024:(ts + 1) * 1024])
                out_dma[cc] = dv
        S.wait("sync", S.ds["sync"], S.ds["sync"].n)

        with nc.Block() as block:
            S.run(block)
    return nc


_CACHE = {}


def kernel(L0_attn_x, L0_attn_Wq, L0_attn_bq, L0_attn_Wk, L0_attn_bk,
           L0_attn_Wv, L0_attn_bv, L0_attn_Wo, L0_attn_bo,
           L0_ffn_W_r, L0_ffn_W_k, L0_ffn_W_v, L0_ffn_mu_r, L0_ffn_mu_k):
    x = np.ascontiguousarray(np.asarray(L0_attn_x, dtype=np.float32))
    consts = _consts16()
    if "p1" not in _CACHE:
        _CACHE["p1"] = build_p1()
        _CACHE["p2"] = build_p2()
    nc1, nc2 = _CACHE["p1"], _CACHE["p2"]

    shards = [x[2 * i: 2 * i + 2] for i in range(8)]
    common1 = {
        "Wq": np.asarray(L0_attn_Wq, np.float32), "bq": np.asarray(L0_attn_bq, np.float32),
        "Wk": np.asarray(L0_attn_Wk, np.float32), "bk": np.asarray(L0_attn_bk, np.float32),
        "Wv": np.asarray(L0_attn_Wv, np.float32), "bv": np.asarray(L0_attn_bv, np.float32),
        "Wo": np.asarray(L0_attn_Wo, np.float32), "bo": np.asarray(L0_attn_bo, np.float32),
        "ident16": consts["ident16"], "ident32": consts["ident32"],
        "onesbf": consts["onesbf"], "ones32": consts["ones32"],
    }
    in1 = [dict(common1, x=s) for s in shards]
    res1 = run_bass_kernel_spmd(nc1, in1, core_ids=list(range(8)))
    allmax = np.concatenate([np.asarray(r["mx"]).reshape(1, 1)
                             for r in res1.results], axis=1)
    common2 = {
        "allmax": allmax,
        "W_r": np.asarray(L0_ffn_W_r, np.float32),
        "W_k": np.asarray(L0_ffn_W_k, np.float32),
        "W_v": np.asarray(L0_ffn_W_v, np.float32),
        "mu_r": np.asarray(L0_ffn_mu_r, np.float32),
        "mu_k": np.asarray(L0_ffn_mu_k, np.float32),
        "ident16": consts["ident16"], "ident32": consts["ident32"],
        "ones16": consts["ones16"],
    }
    in2 = [dict(common2, x=s, y_scr=np.asarray(res1.results[i]["y_scr"]))
           for i, s in enumerate(shards)]
    res2 = run_bass_kernel_spmd(nc2, in2, core_ids=list(range(8)))
    return np.concatenate([np.asarray(r["out"]) for r in res2.results], axis=0)


# version 09
# speedup vs baseline: 1.1862x; 1.1862x over previous
import contextlib
import numpy as np
import ml_dtypes

import concourse.bass as bass
import concourse.mybir as mybir
from concourse.bass_utils import run_bass_kernel_spmd

F32 = mybir.dt.float32
F16 = mybir.dt.float16
BF16 = mybir.dt.bfloat16
U8 = mybir.dt.uint8
OP = mybir.AluOpType
AF = mybir.ActivationFunctionType
AX = mybir.AxisListType.X

B_CORE = 2
T = 2048
C = 1024
NCT = 8
NCH = 4
NTT = 16
NU = 32
MAGIC = float(np.float32(1.5 * 2 ** 23))
EPS = 1e-5

ENGS = ("sync", "scalar", "vector", "tensor", "gpsimd")


class Sem:
    __slots__ = ("h", "n")

    def __init__(self, h):
        self.h = h
        self.n = 0


class Sched:

    def __init__(self, nc, es):
        self.nc = nc
        self.q = {e: [] for e in ENGS}
        self.cs = {e: Sem(es.enter_context(nc.semaphore(f"cs_{e}"))) for e in ENGS}
        self.ds = {e: Sem(es.enter_context(nc.semaphore(f"ds_{e}"))) for e in
                   ("sync", "scalar", "gpsimd")}

    def wait(self, eng, sem, val, future=False):
        if val <= 0:
            return
        assert future or val <= sem.n, f"wait on future count: {val} > {sem.n}"
        self.q[eng].append(lambda e, s=sem.h, v=val: e.wait_ge(s, v))

    def emit(self, eng, fn, inc=True):
        s = self.cs[eng]
        if inc:
            s.n += 1
            self.q[eng].append(lambda e, f=fn, h=s.h: f(e).then_inc(h, 1))
            return s.n
        self.q[eng].append(lambda e, f=fn: f(e))
        return s.n

    def drain(self, eng):
        self.q[eng].append(lambda e: e.drain())

    def dma(self, eng, out, in_, slow=False):
        s = self.ds[eng]
        if s.n > 0:
            self.q[eng].append(lambda e, h=s.h, v=s.n: e.wait_ge(h, v))
        s.n += 16
        self.q[eng].append(
            lambda e, o=out, i=in_, h=s.h, sl=slow: e.dma_start(
                out=o, in_=i, allow_slow_non_contiguous=sl).then_inc(h, 16))
        return s.n

    def run(self, block):
        for eng in ENGS:
            deco = getattr(block, eng)
            ops = self.q[eng]

            def body(handle, _ops=ops):
                for f in _ops:
                    f(handle)

            deco(body)


def _consts16():
    return {
        "ident16": np.eye(128, dtype=np.float16),
        "ident32": np.eye(128, dtype=np.float32),
        "onesbf": np.ones((128, 128), dtype=ml_dtypes.bfloat16),
        "ones32": np.ones((128, 128), dtype=np.float32),
        "ones16": np.ones((128, 128), dtype=np.float16),
    }


def build_p1(debug=False):
    nc = bass.Bass("TRN2", target_bir_lowering=False)
    x_d = nc.dram_tensor("x", [B_CORE, T, C], F32, kind="ExternalInput").ap()
    w_d = {w: nc.dram_tensor(f"W{w}", [C, C], F32, kind="ExternalInput").ap()
           for w in "qkvo"}
    b_d = {w: nc.dram_tensor(f"b{w}", [C], F32, kind="ExternalInput").ap()
           for w in "qkvo"}
    id16_d = nc.dram_tensor("ident16", [128, 128], F16, kind="ExternalInput").ap()
    id32_d = nc.dram_tensor("ident32", [128, 128], F32, kind="ExternalInput").ap()
    onesbf_d = nc.dram_tensor("onesbf", [128, 128], BF16, kind="ExternalInput").ap()
    ones32_d = nc.dram_tensor("ones32", [128, 128], F32, kind="ExternalInput").ap()
    y_d = nc.dram_tensor("y_scr", [B_CORE, C, T], F32, kind="ExternalOutput").ap()
    mx_d = nc.dram_tensor("mx", [1, 1], F32, kind="ExternalOutput").ap()
    tern_scr = nc.dram_tensor("tern_scr", [4, 128, 8192], F16, kind="Internal").ap()
    sml_scr = nc.dram_tensor("sml_scr", [1, 4], F32, kind="Internal").ap()
    dbg = {}
    if debug:
        dbg["alp"] = nc.dram_tensor("dbg_alp", [128, 4], F32, kind="ExternalOutput").ap()
        dbg["tsml"] = nc.dram_tensor("dbg_tsml", [4, 16], F32, kind="ExternalOutput").ap()
        dbg["acc"] = nc.dram_tensor("dbg_acc", [128, 16], F32, kind="ExternalOutput").ap()
        dbg["tern"] = nc.dram_tensor("dbg_tern", [128, 8192], F16, kind="ExternalOutput").ap()
        dbg["ln"] = nc.dram_tensor("dbg_ln", [128, 4096], F16, kind="ExternalOutput").ap()
        dbg["qT"] = nc.dram_tensor("dbg_qT", [128, 16384], F16, kind="ExternalOutput").ap()
        dbg["kT"] = nc.dram_tensor("dbg_kT", [128, 16384], F16, kind="ExternalOutput").ap()
        dbg["v"] = nc.dram_tensor("dbg_v", [128, 16384], BF16, kind="ExternalOutput").ap()
        dbg["E"] = nc.dram_tensor("dbg_E", [128, 16384], BF16, kind="ExternalOutput").ap()
        dbg["yacc"] = nc.dram_tensor("dbg_yacc", [128, 16384], F16, kind="ExternalOutput").ap()

    with contextlib.ExitStack() as es:
        S = Sched(nc, es)
        sb = lambda n, sh, dt: es.enter_context(nc.sbuf_tensor(n, sh, dt))
        ps = lambda n, sh, dt: es.enter_context(nc.psum_tensor(n, sh, dt))
        TEN, VEC, ACT = "tensor", "vector", "scalar"

        U0 = sb("U0", [128, 51200], U8)
        wT = {"q": U0[:, 0:16384].bitcast(F16),
              "k": U0[:, 16384:32768].bitcast(F16),
              "v": U0[:, 32768:49152].bitcast(F16)}
        E_base = U0[:, 0:34816].bitcast(BF16)
        woT = U0[:, 34816:51200].bitcast(F16)

        lnT = sb("lnT", [128, 4096], F16)
        qT = sb("qT", [128, 16384], F16)
        kT = sb("kT", [128, 16384], F16)
        v_sb = sb("v_sb", [128, 16384], BF16)
        y_acc = sb("y_acc", [128, 16384], F16)
        x_stage = sb("x_stage", [128, 2048], F32)
        ln_tok = sb("ln_tok", [128, 1024], F16)
        y_stage = sb("y_stage", [128, 512], F32)
        bvbc = sb("bvbc", [128, 1024], F16)
        rzb = sb("rzb", [128, 512], F16)
        junk = sb("junk", [128, 512], BF16)
        id16 = sb("id16_s", [128, 128], F16)
        id32 = sb("id32_s", [128, 128], F32)
        onesbf = sb("onesbf_s", [128, 128], BF16)
        ones32 = sb("ones32_s", [128, 128], F32)
        bq_sb = sb("bq_sb", [128, 8], F32)
        bk_sb = sb("bk_sb", [128, 8], F32)
        bo_sb = sb("bo_sb", [128, 8], F32)
        alp = sb("alp", [128, 4], F32)
        sm1 = sb("sm1", [128, 16], F32)
        sm2 = sb("sm2", [128, 16], F32)
        mx_run = sb("mx_run", [128, 1], F32)
        mx_out = sb("mx_out", [1, 1], F32)
        t_sml = sb("t_sml", [1, 64], F32)
        acc = sb("acc", [128, 16], F32)
        cvec = sb("cvec", [128, 4], F32)
        spa1 = sb("spa1", [128, 1], F32)
        flv1 = sb("flv1", [128, 16], F32)

        pqkall = ps("pqkall", [128, 2048], F32)
        pav2 = ps("pav2", [128, 1024], F32)
        pz2 = ps("pz2", [128, 1024], F32)
        ex_sem = Sem(es.enter_context(nc.semaphore("ex_sem")))
        ez_sem = Sem(es.enter_context(nc.semaphore("ez_sem")))
        lnz = None
        pqk = pqkall[:, 0:1024]
        pqk2 = pqkall[:, 1024:2048]
        ptr16 = pav2[:, 0:512].bitcast(F16)
        qk_bufs = (pqkall[:, 0:1024], pqkall[:, 1024:2048])
        lnz = ln_tok[:, 0:1024].bitcast(F32)
        lnz2 = sb("lnz2", [128, 512], F32)
        guard16 = pz2[:].bitcast(F16)

        def tp_guard():
            S.emit(TEN, lambda e: e.transpose(guard16[:, 1920:2048], id16[:],
                                              id16[:]), inc=False)

        for col, val in ((0, EPS), (1, 0.125), (2, -1.0), (3, -0.5)):
            S.emit(VEC, lambda e, col=col, val=val: e.memset(cvec[:, col:col + 1], val),
                   inc=(col == 3))
        v_cv = S.cs[VEC].n
        S.wait(ACT, S.cs[VEC], v_cv)
        S.dma("sync", id16[:], id16_d[:])
        S.dma("sync", id32[:], id32_d[:])
        S.dma("sync", onesbf[:], onesbf_d[:])
        v_c32 = S.dma("sync", ones32[:], ones32_d[:])
        S.dma("gpsimd", bq_sb[:], b_d["q"].rearrange("(a p) -> p a", p=128), slow=True)
        S.dma("gpsimd", bk_sb[:], b_d["k"].rearrange("(a p) -> p a", p=128), slow=True)
        v_bo = S.dma("gpsimd", bo_sb[:], b_d["o"].rearrange("(a p) -> p a", p=128), slow=True)
        v_bv = S.dma("gpsimd", bvbc[:], b_d["v"][None, :].partition_broadcast(128)[:, 0, :])
        S.wait(VEC, S.ds["gpsimd"], v_bv)

        if True:
            w_stage = qT[:].bitcast(F32)
            absw = kT[:].bitcast(F32)
            maskb = v_sb[:, 0:8192].bitcast(F16)
            signb = y_acc[:, 0:8192]
            p3scr = v_sb[:, 8192:16384]
            pmisc = pqkall[:, 0:1024]

            v_abs_prev = v_cv
            v_sg_prev = v_tn_prev = v_tt_prev = 0
            for wi, w in enumerate("qkvo"):
                tb = 16 * wi
                ab = 4 * wi
                ts = lambda j, tb=tb: t_sml[0:1, tb + j: tb + j + 1]
                ac = lambda j, ab=ab: acc[:, ab + j: ab + j + 1]
                S.wait("sync", S.cs[VEC], v_abs_prev)
                S.wait("sync", S.cs[ACT], v_sg_prev)
                v_ld = S.dma("sync", w_stage[:].rearrange("p (a c) -> p a c", c=1024),
                             w_d[w].rearrange("(a p) c -> p a c", p=128))
                S.wait(VEC, S.ds["sync"], v_ld)
                S.emit(VEC, lambda e, ac=ac: e.scalar_tensor_tensor(
                    out=absw[:], in0=w_stage[:], scalar=-1.0, in1=w_stage[:],
                    op0=OP.mult, op1=OP.max, accum_out=ac(0)), inc=False)
                S.drain(VEC)
                v_abs = S.emit(VEC, lambda e, ts=ts, ac=ac: e.tensor_copy(
                    out=ts(8), in_=ac(0)[0:1, :]))
                S.wait(TEN, S.ds["sync"], v_c32)
                S.wait(TEN, S.cs[VEC], v_abs)
                tp_guard()
                v_t1 = S.emit(TEN, lambda e, ac=ac, wi=wi: e.transpose(
                    pmisc[0:1, 128 * wi: 128 * wi + 128], ac(0), id32[:]))
                S.wait(VEC, S.cs[TEN], v_t1)
                S.emit(VEC, lambda e, ts=ts, wi=wi: e.tensor_reduce(
                    out=ts(0), in_=pmisc[0:1, 128 * wi: 128 * wi + 128], axis=AX,
                    op=OP.add), inc=False)
                S.drain(VEC)
                S.emit(VEC, lambda e, ts=ts: e.tensor_scalar(
                    out=ts(1), in0=ts(0),
                    scalar1=float(0.05 / (1024.0 * 1024.0)), scalar2=None, op0=OP.mult),
                    inc=False)
                v_dl = S.emit(VEC, lambda e: e.tensor_copy(out=flv1[:], in_=flv1[:]))
                S.wait("gpsimd", S.cs[VEC], v_dl)
                v_db0 = S.dma("gpsimd", sml_scr[0:1, 0:1], ts(1))
                S.wait("gpsimd", S.ds["gpsimd"], v_db0)
                v_db = S.dma("gpsimd", ac(3),
                             sml_scr[0:1, 0:1].partition_broadcast(128)[:, 0, :],
                             slow=True)
                S.wait(VEC, S.ds["gpsimd"], v_db)
                S.emit(VEC, lambda e, ac=ac: e.tensor_scalar(
                    out=maskb[:], in0=absw[:], scalar1=ac(3), scalar2=None,
                    op0=OP.is_gt), inc=False)
                S.drain(VEC)
                S.emit(VEC, lambda e, ac=ac: e.tensor_reduce(
                    out=ac(1), in_=maskb[:], axis=AX, op=OP.add), inc=False)
                S.emit(VEC, lambda e, ac=ac: e.scalar_tensor_tensor(
                    out=p3scr[:], in0=absw[:], scalar=1.0, in1=maskb[:],
                    op0=OP.mult, op1=OP.mult, accum_out=ac(2)), inc=False)
                S.drain(VEC)
                v_ms = S.emit(VEC, lambda e, ts=ts, ac=ac: e.tensor_copy(
                    out=ts(9), in_=ac(2)[0:1, :]))
                S.wait(TEN, S.cs[VEC], v_ms)
                tp_guard()
                S.emit(TEN, lambda e, ac=ac, wi=wi: e.transpose(
                    pmisc[0:1, 512 + 256 * (wi % 2): 512 + 256 * (wi % 2) + 128], ac(1), id32[:]),
                    inc=False)
                v_t2 = S.emit(TEN, lambda e, ac=ac, wi=wi: e.transpose(
                    pmisc[0:1, 512 + 256 * (wi % 2) + 128: 512 + 256 * (wi % 2) + 256], ac(2),
                    id32[:]))
                S.wait(VEC, S.cs[TEN], v_t2)
                S.emit(VEC, lambda e, ts=ts, wi=wi: e.tensor_reduce(
                    out=ts(2), in_=pmisc[0:1, 512 + 256 * (wi % 2): 512 + 256 * (wi % 2) + 128],
                    axis=AX, op=OP.add), inc=False)
                S.emit(VEC, lambda e, ts=ts, wi=wi: e.tensor_reduce(
                    out=ts(3),
                    in_=pmisc[0:1, 512 + 256 * (wi % 2) + 128: 512 + 256 * (wi % 2) + 256],
                    axis=AX, op=OP.add), inc=False)
                S.drain(VEC)
                S.emit(VEC, lambda e, ts=ts: e.tensor_scalar(
                    out=ts(4), in0=ts(2), scalar1=1.0, scalar2=None, op0=OP.max),
                    inc=False)
                S.drain(VEC)
                S.emit(VEC, lambda e, ts=ts: e.reciprocal(out=ts(5), in_=ts(4)),
                       inc=False)
                S.drain(VEC)
                S.emit(VEC, lambda e, ts=ts: e.tensor_tensor(
                    out=ts(6), in0=ts(3), in1=ts(5), op=OP.mult), inc=False)
                v_al = S.emit(VEC, lambda e: e.tensor_copy(out=flv1[:], in_=flv1[:]))
                S.wait("gpsimd", S.cs[VEC], v_al)
                v_al0 = S.dma("gpsimd", sml_scr[0:1, 1:2], ts(6))
                S.wait("gpsimd", S.ds["gpsimd"], v_al0)
                v_alp = S.dma("gpsimd", alp[:, wi:wi + 1],
                              sml_scr[0:1, 1:2].partition_broadcast(128)[:, 0, :],
                              slow=True)
                S.wait(ACT, S.ds["sync"], v_ld)
                S.wait(ACT, S.cs[VEC], v_tn_prev)
                S.wait(ACT, S.cs[TEN], v_tt_prev)
                S.emit(ACT, lambda e: e.activation(out=signb[:],
                                                   in_=w_stage[:], func=AF.Sign),
                       inc=False)
                v_sg = S.emit(ACT, lambda e: e.activation(
                    out=spa1[:], in_=cvec[:, 0:1], func=AF.Identity))
                S.wait(VEC, S.cs[ACT], v_sg)
                v_tn = S.emit(VEC, lambda e: e.scalar_tensor_tensor(
                    out=signb[:], in0=signb[:], scalar=1.0, in1=maskb[:],
                    op0=OP.mult, op1=OP.mult))
                dst = wT[w] if w != "o" else absw[:, 0:4096].bitcast(F16)
                S.wait(TEN, S.cs[VEC], v_tn)
                tp_guard()
                for ot in range(8):
                    v_tt = S.emit(TEN, lambda e, ot=ot: [
                        e.transpose(ptr16[:, ct * 128:(ct + 1) * 128],
                                    signb[:, ot * 1024 + ct * 128: ot * 1024 + (ct + 1) * 128],
                                    id16[:]) for ct in range(8)][-1])
                    S.wait(VEC, S.cs[TEN], v_tt)
                    v_ev = S.emit(VEC, lambda e, ot=ot, dst=dst: e.tensor_copy(
                        out=dst.rearrange("p (ct f) -> p ct f", f=1024)[:, :, ot * 128:(ot + 1) * 128],
                        in_=ptr16[:].rearrange("p (ct f) -> p ct f", f=128)))
                    S.wait(TEN, S.cs[VEC], v_ev)
                v_ev = S.emit(VEC, lambda e: e.tensor_copy(out=flv1[:], in_=flv1[:]))
                S.wait("sync", S.cs[VEC], v_ev)
                v_spill = S.dma("sync", tern_scr[wi], dst)
                if debug:
                    S.wait("sync", S.ds["gpsimd"], v_alp)
                    S.dma("sync", dbg["tsml"][wi:wi + 1, :], t_sml[0:1, tb:tb + 16])
                    S.dma("sync", dbg["acc"][:, 4 * wi: 4 * wi + 4], acc[:, ab:ab + 4])
                if debug and w == "q":
                    S.dma("sync", dbg["tern"], dst)
                if debug and w == "o":
                    S.dma("sync", dbg["alp"], alp[:])
                v_abs_prev, v_sg_prev, v_tn_prev, v_tt_prev = v_abs, v_sg, v_tn, v_tt
        S.wait(VEC, S.ds["gpsimd"], v_alp)
        v_mx0 = S.emit(VEC, lambda e: e.memset(mx_run[:], 0.0))
        v_bvb = v_mx0

        norm_mark = {}
        sq_mark = {}
        ptr_evac = {}
        proj_evac = {}
        qk_done = {}
        tt_done = {}
        ln_mark = {}
        cp_mark = {}
        ln_done = {}
        av_done = {}
        zlast = {}
        wo_mark = {}
        vproj_vec_of = {}
        vproj_act_of = {}

        def emit_avz(uu, s0, s1):
            b_, u_ = divmod(uu, NU)
            p_, c_ = divmod(u_, NCH)
            hA, hB = 2 * p_, 2 * p_ + 1
            for st in range(s0, s1):
                sl = (uu * NTT + st) % 17
                eA = E_base[:, 0 * 8704 + sl * 512: 0 * 8704 + sl * 512 + 512]
                eB = E_base[:, 1 * 8704 + sl * 512: 1 * 8704 + sl * 512 + 512]
                if st % 4 == 0:
                    S.wait(TEN, ex_sem, uu * NTT + min(st + 4, NTT), future=True)
                pav_u = pav2[:, (uu % 2) * 512:(uu % 2) * 512 + 512]
                pz_u = pz2[:, (uu % 2) * 512:(uu % 2) * 512 + 512]
                if st == 0:
                    S.wait(TEN, S.cs[VEC], tt_done.get(uu - 2, 0))
                    S.wait(TEN, S.cs[ACT], ln_mark.get(uu - 2, 0))
                    if u_ == 0:
                        S.wait(TEN, S.cs[VEC], vproj_vec_of[b_])
                S.emit(TEN, lambda e, st=st, hA=hA, eA=eA, pav_u=pav_u: e.matmul(
                    pav_u[0:64, :],
                    v_sb[:, st * 1024 + 64 * hA: st * 1024 + 64 * hA + 64],
                    eA, start=(st == 0), stop=(st == NTT - 1), tile_position=(0, 0)),
                    inc=False)
                S.emit(TEN, lambda e, st=st, hB=hB, eB=eB, pav_u=pav_u: e.matmul(
                    pav_u[64:128, :],
                    v_sb[:, st * 1024 + 64 * hB: st * 1024 + 64 * hB + 64],
                    eB, start=(st == 0), stop=(st == NTT - 1), tile_position=(0, 64),
                    skip_group_check=True), inc=(st == NTT - 1))
                S.emit(TEN, lambda e, st=st, eA=eA, pz_u=pz_u: e.matmul(
                    pz_u[0:64, :], onesbf[:, 0:64], eA,
                    start=(st == 0), stop=(st == NTT - 1), tile_position=(0, 0)),
                    inc=False)
                ez_sem.n += 1
                S.q[TEN].append(lambda e, st=st, eB=eB, h=ez_sem.h, pz_u=pz_u: e.matmul(
                    pz_u[64:128, :], onesbf[:, 0:64], eB,
                    start=(st == 0), stop=(st == NTT - 1), tile_position=(0, 64),
                    skip_group_check=True).then_inc(h, 1))
            if s1 == NTT:
                av_done[uu] = S.cs[TEN].n
                zlast[uu] = S.emit(TEN, lambda e: e.nop())

        def emit_ln(uu):
            pz_u = pz2[:, (uu % 2) * 512:(uu % 2) * 512 + 512]
            S.wait(ACT, S.cs[TEN], zlast[uu])
            S.wait(ACT, S.cs[VEC], cp_mark.get(uu - 1, 0))
            S.emit(ACT, lambda e, pz_u=pz_u: e.activation(
                out=lnz[:], in_=pz_u[:], func=AF.Ln), inc=False)
            ln_mark[uu] = S.emit(ACT, lambda e: e.activation(
                out=spa1[:], in_=cvec[:, 0:1], func=AF.Identity))

        def emit_lncp(uu):
            S.wait(VEC, S.cs[ACT], ln_mark[uu])
            S.emit(VEC, lambda e: e.tensor_copy(out=lnz2[:], in_=lnz[:]), inc=False)
            cp_mark[uu] = S.emit(VEC, lambda e: e.tensor_copy(out=flv1[:],
                                                              in_=flv1[:]))

        def emit_rz(uu):
            S.wait(ACT, S.cs[VEC], cp_mark[uu])
            S.emit(ACT, lambda e: e.activation(
                out=rzb[:], in_=lnz2[:], func=AF.Exp, scale=cvec[:, 2:3]),
                inc=False)
            ln_done[uu] = S.emit(ACT, lambda e: e.activation(
                out=spa1[:], in_=cvec[:, 0:1], func=AF.Identity))

        def emit_tt(uu):
            b_, u_ = divmod(uu, NU)
            p_, c_ = divmod(u_, NCH)
            yo = p_ * 2048 + c_ * 512
            pav_u = pav2[:, (uu % 2) * 512:(uu % 2) * 512 + 512]
            S.wait(VEC, S.cs[ACT], ln_done[uu])
            S.wait(VEC, S.cs[TEN], av_done[uu])
            S.emit(VEC, lambda e, yo=yo, pav_u=pav_u: e.tensor_tensor(
                out=y_acc[0:64, yo:yo + 512], in0=pav_u[0:64, :],
                in1=rzb[0:64, :], op=OP.mult), inc=False)
            tt_done[uu] = S.emit(VEC, lambda e, yo=yo, pav_u=pav_u: e.tensor_tensor(
                out=y_acc[64:128, yo:yo + 512], in0=pav_u[64:128, :],
                in1=rzb[64:128, :], op=OP.mult))

        for b in range(B_CORE):
            if b > 0:
                S.wait("sync", S.cs[TEN], S.cs[TEN].n)
                S.wait("sync", S.cs[ACT], S.cs[ACT].n)
                for wi, w in enumerate("qkv"):
                    reload_v = S.dma("sync", wT[w], tern_scr[wi])
                S.wait(TEN, S.cs[ACT], S.cs[ACT].n)
                S.wait(TEN, S.cs[VEC], S.cs[VEC].n)
            else:
                S.wait(TEN, S.cs[VEC], v_bvb)

            for c in range(NCH):
                for tt in range(4):
                    g = c * 4 + tt
                    gg = b * NTT + g
                    par = g % 2
                    S.wait("gpsimd", S.cs[VEC], norm_mark.get(gg - 2, 0))
                    S.wait("gpsimd", S.cs[ACT], sq_mark.get(gg - 2, 0))
                    v_x = S.dma("gpsimd", x_stage[:, par * 1024:(par + 1) * 1024],
                                x_d[b, g * 128:(g + 1) * 128, :])
                    S.wait(ACT, S.ds["gpsimd"], v_x)
                    S.emit(ACT, lambda e, par=par: [
                        e.activation(out=junk[:],
                                     in_=x_stage[:, par * 1024: par * 1024 + 512],
                                     func=AF.Square,
                                     accum_out=sm1[:, par * 8: par * 8 + 1]),
                        e.activation(out=junk[:],
                                     in_=x_stage[:, par * 1024 + 512:(par + 1) * 1024],
                                     func=AF.Square,
                                     accum_out=sm1[:, par * 8 + 1: par * 8 + 2])][-1],
                        inc=False)
                    S.drain(ACT)
                    sq_mark[gg] = S.emit(ACT, lambda e, par=par: e.copy(
                        out=sm1[:, par * 8 + 7: par * 8 + 8],
                        in_=sm1[:, par * 8 + 1: par * 8 + 2]))
                    S.wait(VEC, S.ds["gpsimd"], v_x)
                    S.emit(VEC, lambda e, par=par: e.tensor_reduce(
                        out=sm2[:, par * 8: par * 8 + 1],
                        in_=x_stage[:, par * 1024:(par + 1) * 1024], axis=AX,
                        op=OP.add), inc=False)
                    S.drain(VEC)
                    S.emit(VEC, lambda e, par=par: e.tensor_scalar(
                        out=sm2[:, par * 8 + 1: par * 8 + 2],
                        in0=sm2[:, par * 8: par * 8 + 1],
                        scalar1=1.0 / 1024.0, scalar2=None, op0=OP.mult), inc=False)
                    S.drain(VEC)
                    S.emit(VEC, lambda e, par=par: e.tensor_tensor(
                        out=sm2[:, par * 8 + 2: par * 8 + 3],
                        in0=sm2[:, par * 8 + 1: par * 8 + 2],
                        in1=sm2[:, par * 8 + 1: par * 8 + 2], op=OP.mult), inc=False)
                    S.wait(VEC, S.cs[ACT], sq_mark[gg])
                    S.emit(VEC, lambda e, par=par: e.tensor_tensor(
                        out=sm1[:, par * 8 + 2: par * 8 + 3],
                        in0=sm1[:, par * 8: par * 8 + 1],
                        in1=sm1[:, par * 8 + 1: par * 8 + 2], op=OP.add), inc=False)
                    S.drain(VEC)
                    S.emit(VEC, lambda e, par=par: e.scalar_tensor_tensor(
                        out=sm2[:, par * 8 + 3: par * 8 + 4],
                        in0=sm1[:, par * 8 + 2: par * 8 + 3], scalar=1.0 / 1024.0,
                        in1=sm2[:, par * 8 + 2: par * 8 + 3],
                        op0=OP.mult, op1=OP.subtract), inc=False)
                    v_var = S.emit(VEC, lambda e: e.tensor_copy(out=flv1[:],
                                                               in_=flv1[:]))
                    S.wait(ACT, S.cs[VEC], v_var)
                    S.emit(ACT, lambda e, par=par: e.activation(
                        out=sm2[:, par * 8 + 4: par * 8 + 5],
                        in_=sm2[:, par * 8 + 3: par * 8 + 4], func=AF.Ln,
                        bias=cvec[:, 0:1]), inc=False)
                    S.drain(ACT)
                    S.emit(ACT, lambda e, par=par: e.activation(
                        out=sm2[:, par * 8 + 5: par * 8 + 6],
                        in_=sm2[:, par * 8 + 4: par * 8 + 5], func=AF.Exp,
                        scale=cvec[:, 3:4]), inc=False)
                    v_rs = S.emit(ACT, lambda e: e.activation(
                        out=spa1[:], in_=cvec[:, 0:1], func=AF.Identity))
                    S.wait(VEC, S.cs[ACT], v_rs)
                    S.emit(VEC, lambda e, par=par: e.tensor_tensor(
                        out=sm2[:, par * 8 + 6: par * 8 + 7],
                        in0=sm2[:, par * 8 + 1: par * 8 + 2],
                        in1=sm2[:, par * 8 + 5: par * 8 + 6], op=OP.mult), inc=False)
                    S.drain(VEC)
                    S.wait(VEC, S.cs[TEN], ptr_evac.get(gg - 1, (0, 0))[0])
                    norm_mark[gg] = S.emit(VEC, lambda e, par=par: e.tensor_scalar(
                        out=ln_tok[:, 0:1024],
                        in0=x_stage[:, par * 1024:(par + 1) * 1024],
                        scalar1=sm2[:, par * 8 + 5: par * 8 + 6],
                        scalar2=sm2[:, par * 8 + 6: par * 8 + 7],
                        op0=OP.mult, op1=OP.subtract))
                    S.wait(TEN, S.cs[VEC], norm_mark[gg])
                    tp_guard()
                    if debug and gg == 4 and "v_ln" in dbg:
                        S.wait(TEN, S.ds["sync"], dbg["v_ln"])
                    v_tp = S.emit(TEN, lambda e: [
                        e.transpose(ptr16[:, ct * 128:(ct + 1) * 128],
                                    ln_tok[:, ct * 128:(ct + 1) * 128],
                                    id16[:]) for ct in range(8)][-1])
                    S.wait(VEC, S.cs[TEN], v_tp)
                    v_pe = S.emit(VEC, lambda e, tt=tt: e.tensor_copy(
                        out=lnT.rearrange("p (ct f) -> p ct f", f=512)[:, :, tt * 128:(tt + 1) * 128],
                        in_=ptr16[:].rearrange("p (ct f) -> p ct f", f=128)))
                    ptr_evac[gg] = (v_tp, v_pe)
                    S.wait(TEN, S.cs[VEC], v_pe)
                S.wait(VEC, S.cs[TEN], ptr_evac[b * NTT + c * 4 + 3][0])
                v_lnT_sp = S.emit(VEC, lambda e: e.tensor_copy(out=flv1[:],
                                                               in_=flv1[:]))
                ptr_evac[b * NTT + c * 4 + 3] = (ptr_evac[b * NTT + c * 4 + 3][0],
                                                 v_lnT_sp)
                v_lnT = ptr_evac[b * NTT + c * 4 + 3][1]
                if debug and b == 0 and c == 0:
                    S.wait("sync", S.cs[VEC], v_lnT)
                    dbg["v_ln"] = S.dma("sync", dbg["ln"], lnT[:])
                S.wait(TEN, S.cs[VEC], v_lnT)
                if b > 0 and c == 0:
                    S.wait(TEN, S.ds["sync"], reload_v)
                for w, dstT, psum, bias, ai in (("q", qT, pqk, bq_sb, 0),
                                                ("k", kT, pqk2, bk_sb, 1)):
                    for ot in range(8):
                        jj = (b * NCH + c) * 8 + ot
                        half = ot % 2
                        S.wait(TEN, S.cs[ACT], proj_evac.get((w, jj - 2), 0))
                        for ct in range(8):
                            vv = S.emit(TEN, lambda e, ct=ct, ot=ot, half=half, w=w, psum=psum: e.matmul(
                                psum[:, half * 512:(half + 1) * 512],
                                wT[w][:, ct * 1024 + ot * 128: ct * 1024 + (ot + 1) * 128],
                                lnT[:, ct * 512:(ct + 1) * 512],
                                start=(ct == 0), stop=(ct == 7)), inc=(ct == 7))
                        S.wait(ACT, S.cs[TEN], vv)
                        proj_evac[(w, jj)] = S.emit(
                            ACT, lambda e, ot=ot, half=half, dstT=dstT, psum=psum,
                            bias=bias, ai=ai, c=c: e.activation(
                                out=dstT[:, ot * 2048 + c * 512: ot * 2048 + (c + 1) * 512],
                                in_=psum[:, half * 512:(half + 1) * 512],
                                func=AF.Identity,
                                scale=alp[:, ai:ai + 1], bias=bias[:, ot:ot + 1]))
                for ts4 in range(4):
                    st_g = c * 4 + ts4
                    for dc in range(2):
                        jj = (b * NCH + c) * 8 + ts4 * 2 + dc
                        half = (ts4 * 2 + dc) % 2
                        S.wait(TEN, S.cs[VEC], proj_evac.get(("v", jj - 2), 0))
                        if b > 0 and c == 0 and ts4 == 0 and dc == 0:
                            pass
                        for ct in range(8):
                            vv = S.emit(TEN, lambda e, ct=ct, ts4=ts4, dc=dc, half=half: e.matmul(
                                pav2[:, half * 512:(half + 1) * 512],
                                lnT[:, ct * 512 + ts4 * 128: ct * 512 + (ts4 + 1) * 128],
                                wT["v"][:, ct * 1024 + dc * 512: ct * 1024 + (dc + 1) * 512],
                                start=(ct == 0), stop=(ct == 7)), inc=(ct == 7))
                        S.wait(VEC, S.cs[TEN], vv)
                        proj_evac[("v", jj)] = S.emit(
                            VEC, lambda e, st_g=st_g, dc=dc, half=half:
                            e.scalar_tensor_tensor(
                                out=v_sb[:, st_g * 1024 + dc * 512: st_g * 1024 + (dc + 1) * 512],
                                in0=pav2[:, half * 512:(half + 1) * 512],
                                scalar=alp[:, 2:3],
                                in1=bvbc[:, dc * 512:(dc + 1) * 512],
                                op0=OP.mult, op1=OP.add))
            v_proj_pe = S.cs[TEN].n
            v_proj_vec = S.cs[VEC].n
            v_proj_act = S.cs[ACT].n
            if debug and b == 0:
                S.wait("sync", S.cs[VEC], v_proj_vec)
                S.dma("sync", dbg["qT"], qT[:])
                S.dma("sync", dbg["kT"], kT[:])
                S.dma("sync", dbg["v"], v_sb[:])

            S.wait("sync", S.cs[VEC], v_proj_vec)
            v_wo = S.dma("sync", woT, tern_scr[3])

            vproj_vec_of[b] = v_proj_vec
            vproj_act_of[b] = v_proj_act
            def emit_qk(uu, s0, s1):
                b_, u_ = divmod(uu, NU)
                p, c = divmod(u_, NCH)
                for st in range(s0, s1):
                    ss = uu * NTT + st
                    buf = qk_bufs[ss % 2]
                    S.wait(TEN, ex_sem, ss - 1, future=True)
                    if u_ == 0 and st == 0:
                        S.wait(TEN, S.cs[VEC], vproj_vec_of[b_])
                        S.wait(TEN, S.cs[ACT], vproj_act_of[b_])
                    S.emit(TEN, lambda e, st=st, p=p, c=c, buf=buf: e.matmul(
                        buf[:, 0:512],
                        kT[0:64, p * 2048 + st * 128: p * 2048 + (st + 1) * 128],
                        qT[0:64, p * 2048 + c * 512: p * 2048 + (c + 1) * 512],
                        start=True, stop=True, tile_position=(0, 0)), inc=False)
                    qk_done[(uu, st)] = S.emit(TEN, lambda e, st=st, p=p, c=c, buf=buf: e.matmul(
                        buf[:, 512:1024],
                        kT[64:128, p * 2048 + st * 128: p * 2048 + (st + 1) * 128],
                        qT[64:128, p * 2048 + c * 512: p * 2048 + (c + 1) * 512],
                        start=True, stop=True, tile_position=(64, 0)))

            for u in range(NU):
                uu = b * NU + u
                for st_i in range(NTT):
                    emit_qk(uu, st_i, st_i + 1)
                    if u > 0:
                        emit_avz(uu - 1, st_i, st_i + 1)
                for st in range(NTT):
                    if st == 0 and u > 1:
                        emit_rz(uu - 2)
                        emit_tt(uu - 2)
                    ss = uu * NTT + st
                    sl = ss % 17
                    buf = qk_bufs[ss % 2]
                    S.wait(ACT, S.cs[TEN], qk_done[(uu, st)])
                    S.wait(ACT, ez_sem, max(0, ss - 16), future=False)
                    if u == 0 and st == 0:
                        S.wait(ACT, S.cs[TEN], v_proj_pe)
                        S.wait(ACT, S.ds["sync"], v_spill)
                    ex_sem.n += 1
                    S.q[ACT].append(lambda e, sl=sl, buf=buf, h=ex_sem.h: e.activation(
                        out=E_base.rearrange("p (h f) -> p h f", f=8704)[:, :, sl * 512:(sl + 1) * 512],
                        in_=buf[:], func=AF.Exp, scale=cvec[:, 1:2]).then_inc(h, 1))
                if u > 0:
                    emit_ln(uu - 1)
                    emit_lncp(uu - 1)
            emit_avz(b * NU + NU - 1, 0, NTT)
            if NU > 1:
                emit_rz(b * NU + NU - 2)
                emit_tt(b * NU + NU - 2)
            emit_ln(b * NU + NU - 1)
            emit_lncp(b * NU + NU - 1)
            emit_rz(b * NU + NU - 1)
            emit_tt(b * NU + NU - 1)
            if debug and b == 0:
                S.wait("sync", S.cs[VEC], tt_done[b * NU + NU - 1])
                S.dma("sync", dbg["yacc"], y_acc[:])
            S.wait(TEN, S.ds["sync"], v_wo)
            S.wait(TEN, S.cs[ACT], S.cs[ACT].n)
            S.wait(TEN, ex_sem, (b * NU + NU) * NTT)
            S.wait(TEN, S.cs[VEC], tt_done[b * NU + NU - 1])
            for ot in range(8):
                for c in range(NCH):
                    j = b * 32 + ot * NCH + c
                    buf = pqk if j % 2 == 0 else pqk2
                    S.wait(TEN, S.cs[VEC], wo_mark.get((j - 2, "ev"), 0))
                    for ct in range(8):
                        vv = S.emit(TEN, lambda e, ct=ct, ot=ot, c=c, buf=buf: e.matmul(
                            buf[:, 0:512],
                            woT[:, ct * 1024 + ot * 128: ct * 1024 + (ot + 1) * 128],
                            y_acc[:, ct * 2048 + c * 512: ct * 2048 + (c + 1) * 512],
                            start=(ct == 0), stop=(ct == 7)), inc=(ct == 7))
                    S.wait(VEC, S.cs[TEN], vv)
                    S.wait(VEC, S.ds["sync"], wo_mark.get((j - 1, "dma"), 0))
                    S.emit(VEC, lambda e, ot=ot, buf=buf: e.tensor_scalar(
                        out=y_stage[:, 0:512], in0=buf[:, 0:512],
                        scalar1=alp[:, 3:4], scalar2=bo_sb[:, ot:ot + 1],
                        op0=OP.mult, op1=OP.add), inc=False)
                    S.drain(VEC)
                    S.emit(VEC, lambda e: e.reduce_max(
                        out=sm1[:, 15:16], in_=y_stage[:, 0:512],
                        axis=AX, apply_absolute_value=True), inc=False)
                    S.drain(VEC)
                    S.emit(VEC, lambda e: e.tensor_tensor(
                        out=mx_run[:], in0=mx_run[:], in1=sm1[:, 15:16], op=OP.max),
                        inc=False)
                    wo_mark[(j, "ev")] = S.emit(VEC, lambda e: e.tensor_copy(
                        out=flv1[:], in_=flv1[:]))
                    S.wait("sync", S.cs[VEC], wo_mark[(j, "ev")])
                    wo_mark[(j, "dma")] = S.dma(
                        "sync", y_d[b, ot * 128:(ot + 1) * 128, c * 512:(c + 1) * 512],
                        y_stage[:, 0:512])

        S.wait(TEN, S.cs[VEC], S.cs[VEC].n)
        S.wait(TEN, S.cs[ACT], S.cs[ACT].n)
        tp_guard()
        v_tm = S.emit(TEN, lambda e: e.transpose(pz2[0:1, 0:128], mx_run[:], id32[:]))
        S.wait(VEC, S.cs[TEN], v_tm)
        S.emit(VEC, lambda e: e.tensor_reduce(
            out=mx_out[:], in_=pz2[0:1, 0:128], axis=AX, op=OP.max), inc=False)
        v_mx = S.emit(VEC, lambda e: e.tensor_copy(out=flv1[:], in_=flv1[:]))
        S.wait("sync", S.cs[VEC], v_mx)
        S.dma("sync", mx_d[:], mx_out[:])
        S.wait("sync", S.ds["sync"], S.ds["sync"].n)

        with nc.Block() as block:
            S.run(block)
    return nc


def build_p2(debug=False):
    nc = bass.Bass("TRN2", target_bir_lowering=False)
    x_d = nc.dram_tensor("x", [B_CORE, T, C], F32, kind="ExternalInput").ap()
    dbg2 = {}
    if debug:
        for nm, dt_ in (("d_x2T", F32), ("d_xr", F16), ("d_xk", F16),
                        ("d_rbuf", F16), ("d_kabuf", F16)):
            dbg2[nm] = nc.dram_tensor(nm, [8, 128, 4096], dt_,
                                      kind="ExternalOutput").ap()
        dbg2["d_stat"] = nc.dram_tensor("d_stat", [8, 128, 2560], F32,
                                        kind="ExternalOutput").ap()
        dbg2["d_fin"] = nc.dram_tensor("d_fin", [64, 128, 512], F32,
                                       kind="ExternalOutput").ap()
        dbg2["d_ln2"] = nc.dram_tensor("d_ln2", [2, 128, 16392], F16,
                                       kind="ExternalOutput").ap()
    y_d = nc.dram_tensor("y_scr", [B_CORE, C, T], F32, kind="ExternalInput").ap()
    am_d = nc.dram_tensor("allmax", [1, 8], F32, kind="ExternalInput").ap()
    wf_d = {w: nc.dram_tensor(f"W_{w}", [C, C], F32, kind="ExternalInput").ap()
            for w in "rkv"}
    mur_d = nc.dram_tensor("mu_r", [C], F32, kind="ExternalInput").ap()
    muk_d = nc.dram_tensor("mu_k", [C], F32, kind="ExternalInput").ap()
    id16_d = nc.dram_tensor("ident16", [128, 128], F16, kind="ExternalInput").ap()
    id32_d = nc.dram_tensor("ident32", [128, 128], F32, kind="ExternalInput").ap()
    ones16_d = nc.dram_tensor("ones16", [128, 128], F16, kind="ExternalInput").ap()
    out_d = nc.dram_tensor("out", [B_CORE, T, C], F32, kind="ExternalOutput").ap()

    with contextlib.ExitStack() as es:
        S = Sched(nc, es)
        sb = lambda n, sh, dt: es.enter_context(nc.sbuf_tensor(n, sh, dt))
        ps = lambda n, sh, dt: es.enter_context(nc.psum_tensor(n, sh, dt))
        TEN, VEC, ACT = "tensor", "vector", "scalar"

        wfT = {w: sb(f"w{w}T", [128, 8192], F16) for w in "rkv"}
        ln2 = sb("ln2", [128, 16392], F16)
        yb = sb("yb", [128, 4096], F32)
        xT = sb("xT", [128, 4096], F32)
        x2T = sb("x2T", [128, 4096], F32)
        xr = sb("xr", [128, 4096], F16)
        xk = sb("xk", [128, 4096], F16)
        r_buf = sb("r_buf", [128, 4096], F16)
        ka_buf = sb("ka_buf", [128, 4096], F16)
        krelu = sb("krelu", [128, 512], F16)
        x2h = sb("x2h", [128, 512], F16)
        xsqh = sb("xsqh", [128, 512], F16)
        stat = sb("stat", [128, 2560], F32)
        fin = sb("fin", [128, 2048], F32)
        out_stage = sb("out_stage", [128, 4096], F32)
        x_stage = sb("x_stage", [128, 2048], F32)
        id16 = sb("id16_s", [128, 128], F16)
        id32 = sb("id32_s", [128, 128], F32)
        ones16 = sb("ones16_s", [128, 128], F16)
        amx = sb("amx", [128, 8], F32)
        gmax = sb("gmax", [128, 4], F32)
        cvec = sb("cvec", [128, 4], F32)
        mur = sb("mur_s", [128, 8], F32)
        muk = sb("muk_s", [128, 8], F32)
        omur = sb("omur", [128, 8], F32)
        omuk = sb("omuk", [128, 8], F32)

        fl = sb("fl", [128, 256], F32)
        spa = sb("spa", [128, 1], F32)
        ps1 = ps("ps1", [128, 512], F32)
        ps2 = ps("ps2", [128, 512], F32)
        pffn = ps("pffn", [128, 1024], F32)
        ptrp = ps("ptrp", [128, 1024], F32)

        for col, val in ((0, EPS), (1, 0.5), (2, -0.5)):
            S.emit(VEC, lambda e, col=col, val=val: e.memset(cvec[:, col:col + 1], val),
                   inc=(col == 2))
        v_cv = S.cs[VEC].n
        S.wait(ACT, S.cs[VEC], v_cv)
        S.dma("sync", id16[:], id16_d[:])
        S.dma("sync", id32[:], id32_d[:])
        S.dma("sync", ones16[:], ones16_d[:])
        S.dma("scalar", amx[:], am_d[0:1, :].partition_broadcast(128)[:, 0, :])
        S.dma("scalar", mur[:], mur_d.rearrange("(a p) -> p a", p=128), slow=True)
        v_mu = S.dma("scalar", muk[:], muk_d.rearrange("(a p) -> p a", p=128), slow=True)
        S.wait(VEC, S.ds["scalar"], v_mu)
        S.emit(VEC, lambda e: e.reduce_max(out=gmax[:, 0:1], in_=amx[:], axis=AX),
               inc=False)
        S.drain(VEC)
        S.emit(VEC, lambda e: e.tensor_scalar(
            out=gmax[:, 1:2], in0=gmax[:, 0:1], scalar1=1.0 / 7.0, scalar2=1e-8,
            op0=OP.mult, op1=OP.max), inc=False)
        S.drain(VEC)
        v_sc = S.emit(VEC, lambda e: e.reciprocal(out=gmax[:, 2:3], in_=gmax[:, 1:2]))
        S.wait(VEC, S.ds["scalar"], v_mu)
        S.emit(VEC, lambda e: e.tensor_scalar(
            out=omur[:], in0=mur[:], scalar1=-1.0, scalar2=1.0, op0=OP.mult,
            op1=OP.add), inc=False)
        v_mu2 = S.emit(VEC, lambda e: e.tensor_scalar(
            out=omuk[:], in0=muk[:], scalar1=-1.0, scalar2=1.0, op0=OP.mult,
            op1=OP.add))

        with contextlib.ExitStack() as ses:
            wstage = ln2[:, 0:8192]
            ptr16 = ses.enter_context(nc.psum_tensor("ptr16", [128, 1024], F16))
            ev_mark = 0
            v_dma_prev = 0
            for w in "rkv":
                S.wait("gpsimd", S.cs[TEN], v_dma_prev)
                v_ld = S.dma("gpsimd", wstage.rearrange("p (a c) -> p a c", c=1024),
                             wf_d[w].rearrange("(a p) c -> p a c", p=128))
                S.wait(TEN, S.ds["gpsimd"], v_ld)
                for ot in range(8):
                    v_tt = S.emit(TEN, lambda e, ot=ot: [
                        e.transpose(ptr16[:, ct * 128:(ct + 1) * 128],
                                    wstage[:, ot * 1024 + ct * 128: ot * 1024 + (ct + 1) * 128],
                                    id16[:]) for ct in range(8)][-1])
                    S.wait(VEC, S.cs[TEN], v_tt)
                    ev_mark = S.emit(VEC, lambda e, ot=ot, w=w: e.tensor_copy(
                        out=wfT[w].rearrange("p (ct f) -> p ct f", f=1024)[:, :, ot * 128:(ot + 1) * 128],
                        in_=ptr16[:].rearrange("p (ct f) -> p ct f", f=128)))
                    S.wait(TEN, S.cs[VEC], ev_mark)
                v_dma_prev = S.cs[TEN].n
        v_w_t = S.cs[TEN].n

        fin_dump = {}
        x2t_dump = {}
        x_mark = {}
        yb_free = {}
        x2_free = {}
        fin_mark = {}
        out_ev = {}
        out_dma = {}
        r_psum_rd = {}
        r_ts = {}
        k_psum_rd = {}
        o_psum_rd = {}
        for b in range(B_CORE):
            if b == 0:
                S.wait(VEC, S.cs[TEN], v_w_t)
            v_z0 = S.emit(VEC, lambda e: e.memset(
                ln2.rearrange("p (ct f) -> p ct f", f=2049)[:, :, 0:1], 0.0))
            for c in range(NCH):
                cc = b * NCH + c
                S.wait("sync", S.cs[VEC], yb_free.get(cc - 1, 0))
                v_y = S.dma("sync", yb.rearrange("p (ct f) -> p ct f", f=512),
                            y_d[b].rearrange("(ct p) t -> p ct t", p=128)[:, :, c * 512:(c + 1) * 512])
                for tt in range(4):
                    gg = cc * 4 + tt
                    par = tt % 2
                    S.wait("sync", S.cs[TEN], x_mark.get(gg - 2, 0))
                    v_x = S.dma("sync", x_stage[:, par * 1024:(par + 1) * 1024],
                                x_d[b, (c * 4 + tt) * 128:(c * 4 + tt + 1) * 128, :])
                    S.wait(TEN, S.ds["sync"], v_x)
                    if gg == 0:
                        S.wait(TEN, S.cs[TEN], v_w_t)
                    S.wait(TEN, S.cs[VEC], x2_free.get(cc - 1, 0))
                    S.wait(TEN, S.cs[VEC], out_ev.get((cc - 1) * 8 + 7, 0))
                    x_mark[gg] = S.emit(TEN, lambda e, par=par: [
                        e.transpose(ptrp[:, (ct // 4) * 512 + (ct % 4) * 128:
                                         (ct // 4) * 512 + (ct % 4) * 128 + 128],
                                    x_stage[:, par * 1024 + ct * 128: par * 1024 + (ct + 1) * 128],
                                    id32[:]) for ct in range(8)][-1])
                    S.wait(VEC, S.cs[TEN], x_mark[gg])
                    S.emit(VEC, lambda e, tt=tt: e.tensor_copy(
                        out=xT.rearrange("p (ct f) -> p ct f", f=512)[:, 0:4, tt * 128:(tt + 1) * 128],
                        in_=ptrp[:, 0:512].rearrange("p (ct f) -> p ct f", f=128)),
                        inc=False)
                    v_xe = S.emit(VEC, lambda e, tt=tt: e.tensor_copy(
                        out=xT.rearrange("p (ct f) -> p ct f", f=512)[:, 4:8, tt * 128:(tt + 1) * 128],
                        in_=ptrp[:, 512:1024].rearrange("p (ct f) -> p ct f", f=128)))
                    S.wait(TEN, S.cs[VEC], v_xe)
                S.wait(VEC, S.ds["sync"], v_y)
                S.wait(VEC, S.cs[VEC], v_sc)
                if debug:
                    S.wait(VEC, S.ds["sync"], x2t_dump.get(cc - 1, 0))
                for ct in range(8):
                    sl = slice(ct * 512, (ct + 1) * 512)
                    S.emit(VEC, lambda e, sl=sl: e.tensor_scalar(
                        out=x2T[:, sl], in0=yb[:, sl], scalar1=gmax[:, 2:3],
                        scalar2=MAGIC, op0=OP.mult, op1=OP.add), inc=False)
                S.drain(VEC)
                for ct in range(8):
                    sl = slice(ct * 512, (ct + 1) * 512)
                    S.emit(VEC, lambda e, sl=sl: e.tensor_scalar(
                        out=yb[:, sl], in0=x2T[:, sl], scalar1=MAGIC, scalar2=7.0,
                        op0=OP.subtract, op1=OP.min), inc=False)
                S.drain(VEC)
                for ct in range(8):
                    sl = slice(ct * 512, (ct + 1) * 512)
                    S.emit(VEC, lambda e, sl=sl: e.tensor_scalar(
                        out=yb[:, sl], in0=yb[:, sl], scalar1=-7.0,
                        scalar2=gmax[:, 1:2], op0=OP.max, op1=OP.mult), inc=False)
                S.drain(VEC)
                for ct in range(8):
                    sl = slice(ct * 512, (ct + 1) * 512)
                    S.emit(VEC, lambda e, sl=sl: e.tensor_tensor(
                        out=x2T[:, sl], in0=yb[:, sl], in1=xT[:, sl], op=OP.add),
                        inc=False)
                S.drain(VEC)
                if debug:
                    S.wait("sync", S.cs[VEC], S.cs[VEC].n)
                    x2t_dump[cc] = S.dma("sync", dbg2["d_x2T"][cc], x2T[:])
                st_mm = 0
                mixd = r_buf
                mixtr = xT
                mixtk = yb
                v_st = {}
                x2h_p = (x2h[:, 0:512], krelu[:, 0:512])
                xsqh_p = (xsqh[:, 0:512], fin[:, 0:256].bitcast(F16))
                for ct in range(9):
                    if ct < 8:
                        sl = slice(ct * 512, (ct + 1) * 512)
                        par = ct % 2
                        S.wait(VEC, S.cs[TEN], st_mm)
                        S.emit(VEC, lambda e, sl=sl, par=par: e.tensor_copy(
                            out=x2h_p[par], in_=x2T[:, sl]), inc=False)
                        v_st[ct] = S.emit(VEC, lambda e, sl=sl, par=par: e.tensor_tensor(
                            out=xsqh_p[par], in0=x2T[:, sl], in1=x2T[:, sl],
                            op=OP.mult))
                    if ct >= 1:
                        pc = ct - 1
                        par = pc % 2
                        S.wait(TEN, S.cs[VEC], v_st[pc])
                        S.emit(TEN, lambda e, pc=pc, par=par: e.matmul(
                            ps1[:], ones16[:], x2h_p[par],
                            start=(pc == 0), stop=(pc == 7)), inc=False)
                        st_mm = S.emit(TEN, lambda e, pc=pc, par=par: e.matmul(
                            ps2[:], ones16[:], xsqh_p[par],
                            start=(pc == 0), stop=(pc == 7)))
                S.wait(VEC, S.cs[TEN], st_mm)
                S.emit(VEC, lambda e: e.tensor_scalar(
                    out=stat[:, 0:512], in0=ps1[:], scalar1=1.0 / 1024.0,
                    scalar2=None, op0=OP.mult), inc=False)
                S.drain(VEC)
                S.emit(VEC, lambda e: e.tensor_tensor(
                    out=stat[:, 512:1024], in0=stat[:, 0:512], in1=stat[:, 0:512],
                    op=OP.mult), inc=False)
                S.drain(VEC)
                S.emit(VEC, lambda e: e.scalar_tensor_tensor(
                    out=stat[:, 1024:1536], in0=ps2[:], scalar=1.0 / 1024.0,
                    in1=stat[:, 512:1024], op0=OP.mult, op1=OP.subtract), inc=False)
                v_var = S.emit(VEC, lambda e: e.tensor_copy(out=fl[:], in_=fl[:]))
                S.wait(ACT, S.cs[VEC], v_var)
                S.emit(ACT, lambda e: e.activation(
                    out=stat[:, 512:1024], in_=stat[:, 1024:1536], func=AF.Ln,
                    bias=cvec[:, 0:1]), inc=False)
                S.drain(ACT)
                S.emit(ACT, lambda e: e.activation(
                    out=stat[:, 1536:2048], in_=stat[:, 512:1024], func=AF.Exp,
                    scale=cvec[:, 2:3]), inc=False)
                v_rstd = S.emit(ACT, lambda e: e.activation(
                    out=spa[:], in_=cvec[:, 0:1], func=AF.Identity))
                S.wait(VEC, S.cs[ACT], v_rstd)
                v_mrs = S.emit(VEC, lambda e: e.tensor_tensor(
                    out=stat[:, 2048:2560], in0=stat[:, 0:512],
                    in1=stat[:, 1536:2048], op=OP.mult))
                S.drain(VEC)
                for ct in range(8):
                    sl = slice(ct * 512, (ct + 1) * 512)
                    S.emit(VEC, lambda e, sl=sl: e.tensor_tensor(
                        out=yb[:, sl], in0=x2T[:, sl], in1=stat[:, 1536:2048],
                        op=OP.mult), inc=False)
                S.drain(VEC)
                for ct in range(8):
                    sl = slice(ct * 512, (ct + 1) * 512)
                    dsl = slice(ct * 2049 + 1 + c * 512, ct * 2049 + 1 + (c + 1) * 512)
                    v_l2 = S.emit(VEC, lambda e, sl=sl, dsl=dsl: e.tensor_tensor(
                        out=ln2[:, dsl], in0=yb[:, sl], in1=stat[:, 2048:2560],
                        op=OP.subtract), inc=(ct == 7))
                pass
                S.wait(VEC, S.cs[VEC], v_z0)
                S.drain(VEC)
                for ct in range(8):
                    psl = slice(ct * 2049 + c * 512, ct * 2049 + (c + 1) * 512)
                    dsl = slice(ct * 2049 + 1 + c * 512, ct * 2049 + 1 + (c + 1) * 512)
                    S.emit(VEC, lambda e, psl=psl, dsl=dsl, ct=ct: e.tensor_tensor(
                        out=mixd[:, ct * 512:(ct + 1) * 512],
                        in0=ln2[:, psl], in1=ln2[:, dsl], op=OP.subtract),
                        inc=False)
                S.drain(VEC)
                for ct in range(8):
                    for tmp, omu_v in ((mixtr, omur), (mixtk, omuk)):
                        S.emit(VEC, lambda e, tmp=tmp, omu_v=omu_v, ct=ct:
                               e.tensor_scalar(
                                   out=tmp[:, ct * 512:(ct + 1) * 512],
                                   in0=mixd[:, ct * 512:(ct + 1) * 512],
                                   scalar1=omu_v[:, ct:ct + 1],
                                   scalar2=None, op0=OP.mult), inc=False)
                S.drain(VEC)
                for ct in range(8):
                    dsl = slice(ct * 2049 + 1 + c * 512, ct * 2049 + 1 + (c + 1) * 512)
                    for dst, tmp in ((xr, mixtr), (xk, mixtk)):
                        v_mix = S.emit(VEC, lambda e, dsl=dsl, dst=dst, tmp=tmp, ct=ct:
                                       e.tensor_tensor(
                                           out=dst[:, ct * 512:(ct + 1) * 512],
                                           in0=tmp[:, ct * 512:(ct + 1) * 512],
                                           in1=ln2[:, dsl], op=OP.add),
                                       inc=(ct == 7 and dst is xk))
                if debug:
                    S.wait("sync", S.cs[VEC], v_mix)
                    S.dma("sync", dbg2["d_xr"][cc], xr[:])
                    S.dma("sync", dbg2["d_xk"][cc], xk[:])
                    S.dma("sync", dbg2["d_stat"][cc], stat[:])
                v_mix = S.emit(VEC, lambda e: e.tensor_copy(out=fl[:], in_=fl[:]))
                yb_free[cc] = v_mix
                if debug and c == NCH - 1:
                    S.wait("gpsimd", S.cs[VEC], v_mix)
                    S.dma("gpsimd", dbg2["d_ln2"][b], ln2[:])
                S.wait(TEN, S.cs[VEC], v_mix)
                for dt in range(8):
                    half = dt % 2
                    S.wait(TEN, S.cs[ACT], r_psum_rd.get((cc, dt - 2), 0))
                    for ct in range(8):
                        v_r = S.emit(TEN, lambda e, dt=dt, ct=ct, half=half: e.matmul(
                            pffn[:, half * 512:(half + 1) * 512],
                            wfT["r"][:, ct * 1024 + dt * 128: ct * 1024 + (dt + 1) * 128],
                            xr[:, ct * 512:(ct + 1) * 512],
                            start=(ct == 0), stop=(ct == 7)), inc=(ct == 7))
                    S.wait(ACT, S.cs[TEN], v_r)
                    S.wait(ACT, S.cs[VEC], r_ts.get(cc * 8 + dt - 1, 0))
                    r_psum_rd[(cc, dt)] = S.emit(ACT, lambda e, half=half: e.activation(
                        out=krelu[:], in_=pffn[:, half * 512:(half + 1) * 512],
                        func=AF.Tanh, scale=cvec[:, 1:2]))
                    S.wait(VEC, S.cs[ACT], r_psum_rd[(cc, dt)])
                    r_ts[cc * 8 + dt] = S.emit(VEC, lambda e, dt=dt: e.tensor_scalar(
                        out=r_buf[:, dt * 512:(dt + 1) * 512], in0=krelu[:],
                        scalar1=0.5, scalar2=0.5, op0=OP.mult, op1=OP.add))
                if debug:
                    S.wait("sync", S.cs[VEC], r_ts[cc * 8 + 7])
                    S.dma("sync", dbg2["d_rbuf"][cc], r_buf[:])
                S.wait(TEN, S.cs[ACT], r_psum_rd[(cc, 6)])
                S.wait(TEN, S.cs[ACT], r_psum_rd[(cc, 7)])
                for dt in range(8):
                    half = dt % 2
                    S.wait(TEN, S.cs[VEC], k_psum_rd.get((cc, dt - 2), 0))
                    for ct in range(8):
                        v_k = S.emit(TEN, lambda e, dt=dt, ct=ct, half=half: e.matmul(
                            pffn[:, half * 512:(half + 1) * 512],
                            wfT["k"][:, ct * 1024 + dt * 128: ct * 1024 + (dt + 1) * 128],
                            xk[:, ct * 512:(ct + 1) * 512],
                            start=(ct == 0), stop=(ct == 7)), inc=(ct == 7))
                    S.wait(VEC, S.cs[TEN], v_k)
                    S.emit(VEC, lambda e, half=half: e.tensor_scalar(
                        out=krelu[:], in0=pffn[:, half * 512:(half + 1) * 512],
                        scalar1=0.0, scalar2=None, op0=OP.max), inc=False)
                    S.drain(VEC)
                    k_psum_rd[(cc, dt)] = S.emit(VEC, lambda e, dt=dt: e.tensor_tensor(
                        out=ka_buf[:, dt * 512:(dt + 1) * 512], in0=krelu[:],
                        in1=krelu[:], op=OP.mult))
                k_psum_rd[(cc, 7)] = S.emit(VEC, lambda e: e.tensor_copy(
                    out=fl[:], in_=fl[:]))
                if debug:
                    S.wait("sync", S.cs[VEC], k_psum_rd[(cc, 7)])
                    S.dma("sync", dbg2["d_kabuf"][cc], ka_buf[:])
                S.wait(TEN, S.cs[VEC], k_psum_rd[(cc, 6)])
                S.wait(TEN, S.cs[VEC], k_psum_rd[(cc, 7)])
                for dt in range(9):
                    if dt < 8:
                        half = dt % 2
                        fpar = dt % 2
                        S.wait(TEN, S.cs[VEC], o_psum_rd.get((cc, dt - 2), 0))
                        for ct in range(8):
                            v_o = S.emit(TEN, lambda e, dt=dt, ct=ct, half=half: e.matmul(
                                pffn[:, half * 512:(half + 1) * 512],
                                wfT["v"][:, ct * 1024 + dt * 128: ct * 1024 + (dt + 1) * 128],
                                ka_buf[:, ct * 512:(ct + 1) * 512],
                                start=(ct == 0), stop=(ct == 7)), inc=(ct == 7))
                        S.wait(VEC, S.cs[TEN], v_o)
                        S.wait(VEC, S.cs[TEN], fin_mark.get((cc, dt - 2), 0))
                        S.emit(VEC, lambda e, dt=dt, half=half, fpar=fpar:
                               e.scalar_tensor_tensor(
                                   out=fin[:, fpar * 512:(fpar + 1) * 512],
                                   in0=pffn[:, half * 512:(half + 1) * 512], scalar=1.0,
                                   in1=r_buf[:, dt * 512:(dt + 1) * 512],
                                   op0=OP.mult, op1=OP.mult), inc=False)
                        S.drain(VEC)
                        if debug:
                            S.wait(VEC, S.ds["gpsimd"], fin_dump.get((cc, dt - 2), 0))
                        o_psum_rd[(cc, dt)] = S.emit(VEC, lambda e, dt=dt, fpar=fpar:
                               e.tensor_tensor(
                                   out=fin[:, 1024 + fpar * 512: 1024 + (fpar + 1) * 512],
                                   in0=fin[:, fpar * 512:(fpar + 1) * 512],
                                   in1=x2T[:, dt * 512:(dt + 1) * 512],
                                   op=OP.add))
                        if debug:
                            S.wait("gpsimd", S.cs[VEC], o_psum_rd[(cc, dt)])
                            fin_dump[(cc, dt)] = S.dma(
                                "gpsimd", dbg2["d_fin"][cc * 8 + dt],
                                fin[:, 1024 + fpar * 512: 1024 + (fpar + 1) * 512])
                    if dt >= 1:
                        pdt = dt - 1
                        ppar = pdt % 2
                        ff = cc * 8 + pdt
                        S.wait(TEN, S.cs[VEC], o_psum_rd[(cc, pdt)])
                        S.wait(TEN, S.cs[VEC], out_ev.get(ff - 2, 0))
                        fin_mark[(cc, pdt)] = S.emit(TEN, lambda e, ppar=ppar: [
                            e.transpose(ptrp[:, ppar * 512 + ts * 128: ppar * 512 + (ts + 1) * 128],
                                        fin[:, 1024 + ppar * 512 + ts * 128: 1024 + ppar * 512 + (ts + 1) * 128],
                                        id32[:]) for ts in range(4)][-1])
                        S.wait(VEC, S.cs[TEN], fin_mark[(cc, pdt)])
                        S.wait(VEC, S.ds["sync"], out_dma.get(cc - 1, 0))
                        out_ev[ff] = S.emit(VEC, lambda e, pdt=pdt, ppar=ppar:
                                            e.tensor_copy(
                                                out=out_stage.rearrange("p (ts f) -> p ts f", f=1024)[:, :, pdt * 128:(pdt + 1) * 128],
                                                in_=ptrp[:, ppar * 512:(ppar + 1) * 512].rearrange("p (ts f) -> p ts f", f=128)))
                x2_free[cc] = o_psum_rd[(cc, 7)]
                out_ev[cc * 8 + 7] = S.emit(VEC, lambda e: e.tensor_copy(
                    out=fl[:], in_=fl[:]))
                S.wait("sync", S.cs[VEC], out_ev[cc * 8 + 7])
                for ts in range(4):
                    dv = S.dma("sync", out_d[b, c * 512 + ts * 128: c * 512 + (ts + 1) * 128, :],
                               out_stage[:, ts * 1024:(ts + 1) * 1024])
                out_dma[cc] = dv
        S.wait("sync", S.ds["sync"], S.ds["sync"].n)

        with nc.Block() as block:
            S.run(block)
    return nc


_CACHE = {}


def kernel(L0_attn_x, L0_attn_Wq, L0_attn_bq, L0_attn_Wk, L0_attn_bk,
           L0_attn_Wv, L0_attn_bv, L0_attn_Wo, L0_attn_bo,
           L0_ffn_W_r, L0_ffn_W_k, L0_ffn_W_v, L0_ffn_mu_r, L0_ffn_mu_k):
    x = np.ascontiguousarray(np.asarray(L0_attn_x, dtype=np.float32))
    consts = _consts16()
    if "p1" not in _CACHE:
        _CACHE["p1"] = build_p1()
        _CACHE["p2"] = build_p2()
    nc1, nc2 = _CACHE["p1"], _CACHE["p2"]

    shards = [x[2 * i: 2 * i + 2] for i in range(8)]
    common1 = {
        "Wq": np.asarray(L0_attn_Wq, np.float32), "bq": np.asarray(L0_attn_bq, np.float32),
        "Wk": np.asarray(L0_attn_Wk, np.float32), "bk": np.asarray(L0_attn_bk, np.float32),
        "Wv": np.asarray(L0_attn_Wv, np.float32), "bv": np.asarray(L0_attn_bv, np.float32),
        "Wo": np.asarray(L0_attn_Wo, np.float32), "bo": np.asarray(L0_attn_bo, np.float32),
        "ident16": consts["ident16"], "ident32": consts["ident32"],
        "onesbf": consts["onesbf"], "ones32": consts["ones32"],
    }
    in1 = [dict(common1, x=s) for s in shards]
    res1 = run_bass_kernel_spmd(nc1, in1, core_ids=list(range(8)))
    allmax = np.concatenate([np.asarray(r["mx"]).reshape(1, 1)
                             for r in res1.results], axis=1)
    common2 = {
        "allmax": allmax,
        "W_r": np.asarray(L0_ffn_W_r, np.float32),
        "W_k": np.asarray(L0_ffn_W_k, np.float32),
        "W_v": np.asarray(L0_ffn_W_v, np.float32),
        "mu_r": np.asarray(L0_ffn_mu_r, np.float32),
        "mu_k": np.asarray(L0_ffn_mu_k, np.float32),
        "ident16": consts["ident16"], "ident32": consts["ident32"],
        "ones16": consts["ones16"],
    }
    in2 = [dict(common2, x=s, y_scr=np.asarray(res1.results[i]["y_scr"]))
           for i, s in enumerate(shards)]
    res2 = run_bass_kernel_spmd(nc2, in2, core_ids=list(range(8)))
    return np.concatenate([np.asarray(r["out"]) for r in res2.results], axis=0)
